# version 12
# baseline (speedup 1.0000x reference)
import numpy as np
import ml_dtypes
from contextlib import ExitStack

import concourse.bass as bass
import concourse.mybir as mybir
from concourse.bass_utils import run_bass_kernel_spmd

F32 = mybir.dt.float32
BF16 = mybir.dt.bfloat16
AF = mybir.ActivationFunctionType
ALU = mybir.AluOpType
AX = mybir.AxisListType

D = 2048
CT = D // 128
H = 16
E = 32
I = 768
ITn = I // 128
TOPK = 4
EPS = 1e-5
BIG = 1.0e5
CAP = 256
BIGS = 384.0
CT0G = 768 // 128


def build_nc(T, dbg=False):
    NH = T // 2
    G = NH // 8
    assert NH <= 512 and NH % 128 == 0
    nc = bass.Bass("TRN2", target_bir_lowering=False, detect_race_conditions=False)

    xT = nc.dram_tensor("xT", [D, T], F32, kind="ExternalInput")
    wq = nc.dram_tensor("wq", [D, D], F32, kind="ExternalInput")
    wk = nc.dram_tensor("wk", [D, D], F32, kind="ExternalInput")
    wv = nc.dram_tensor("wv", [D, D], F32, kind="ExternalInput")
    wo = nc.dram_tensor("wo", [D, D], F32, kind="ExternalInput")
    rwT = nc.dram_tensor("rwT", [D, E], F32, kind="ExternalInput")
    gT = nc.dram_tensor("gT", [E, D, I], F32, kind="ExternalInput")
    uT = nc.dram_tensor("uT", [E, D, I], F32, kind="ExternalInput")
    dT = nc.dram_tensor("dT", [E, I, D], F32, kind="ExternalInput")
    c_idb = nc.dram_tensor("c_idb", [128, 128], BF16, kind="ExternalInput")
    c_idf = nc.dram_tensor("c_idf", [128, 128], F32, kind="ExternalInput")
    c_1f = nc.dram_tensor("c_1f", [128, 128], F32, kind="ExternalInput")
    c_bd = nc.dram_tensor("c_bd", [128, 128], BF16, kind="ExternalInput")
    c_ltf = nc.dram_tensor("c_ltf", [128, 128], F32, kind="ExternalInput")
    c_iota = nc.dram_tensor("c_iota", [128, CAP], BF16, kind="ExternalInput")
    c_ioc = nc.dram_tensor("c_ioc", [128, 1], F32, kind="ExternalInput")
    outT = nc.dram_tensor("outT", [D, T], F32, kind="ExternalOutput")
    if dbg:
        d_xn = nc.dram_tensor("d_xn", [128, CT * T], BF16, kind="ExternalOutput")
        d_qt = nc.dram_tensor("d_qt", [128, H * (T // 2)], BF16, kind="ExternalOutput")
        d_kt = nc.dram_tensor("d_kt", [128, H * (T // 2)], BF16, kind="ExternalOutput")
        d_vt = nc.dram_tensor("d_vt", [128, H * (T // 2)], BF16, kind="ExternalOutput")
        d_ot = nc.dram_tensor("d_ot", [128, H * (T // 2)], BF16, kind="ExternalOutput")
        d_x1a = nc.dram_tensor("d_x1a", [128, CT * T], F32, kind="ExternalOutput")
        d_lgt = nc.dram_tensor("d_lgt", [128, (T // 128) * E], F32, kind="ExternalOutput")
        d_ew = nc.dram_tensor("d_ew", [128, (T // 128) * E], F32, kind="ExternalOutput")
        d_selm = nc.dram_tensor("d_selm", [128, (T // 128) * E], F32, kind="ExternalOutput")
        d_mneg = nc.dram_tensor("d_mneg", [128, T // 128], F32, kind="ExternalOutput")
        d_wtm = nc.dram_tensor("d_wtm", [128, (T // 128) * E], BF16, kind="ExternalOutput")

    xT_v = xT.ap().rearrange("(ct p) t -> ct p t", p=128)
    outT_v = outT.ap().rearrange("(ct p) t -> ct p t", p=128)
    wq_v = wq.ap().rearrange("(ct p) j -> p ct j", p=128)
    wk_v = wk.ap().rearrange("(ct p) j -> p ct j", p=128)
    wv_v = wv.ap().rearrange("(ct p) j -> p ct j", p=128)
    wo_v = wo.ap().rearrange("(ct p) j -> p ct j", p=128)
    rwT_v = rwT.ap().rearrange("(ct p) e -> p ct e", p=128)
    gT_v = gT.ap().rearrange("e (ct p) i -> e p ct i", p=128)
    uT_v = uT.ap().rearrange("e (ct p) i -> e p ct i", p=128)
    dT_v = dT.ap().rearrange("e (it p) j -> e p it j", p=128)

    with ExitStack() as outer:
        oe = outer.enter_context
        x1 = oe(nc.sbuf_tensor("x1", [128, CT * T], F32))
        xn = oe(nc.sbuf_tensor("xn", [128, CT * T], BF16))
        idb = oe(nc.sbuf_tensor("idb", [128, 128], BF16))
        idf = oe(nc.sbuf_tensor("idf", [128, 128], F32))
        onesf = oe(nc.sbuf_tensor("onesf", [128, 128], F32))
        bdm = oe(nc.sbuf_tensor("bdm", [128, 128], BF16))
        RW = oe(nc.sbuf_tensor("RW", [128, CT * E], BF16))
        onesb = oe(nc.sbuf_tensor("onesb", [128, 128], BF16))
        ltf = oe(nc.sbuf_tensor("ltf", [128, 128], F32))
        iota = oe(nc.sbuf_tensor("iota", [128, CAP], BF16))
        ioc = oe(nc.sbuf_tensor("ioc", [128, 1], F32))
        xsub = oe(nc.sbuf_tensor("xsub", [128, 512], F32))
        epsb = oe(nc.sbuf_tensor("epsb", [128, 1], F32))
        lg_sb = oe(nc.sbuf_tensor("lg_sb", [E, T], F32))
        lgt = oe(nc.sbuf_tensor("lgt", [128, (T // 128) * E], F32))
        ew = oe(nc.sbuf_tensor("ew", [128, (T // 128) * E], F32))
        ework = oe(nc.sbuf_tensor("ework", [128, (T // 128) * E], F32))
        hitb = oe(nc.sbuf_tensor("hitb", [128, (T // 128) * E], F32))
        selm = oe(nc.sbuf_tensor("selm", [128, (T // 128) * E], F32))
        mneg = oe(nc.sbuf_tensor("mneg", [128, T // 128], F32))
        mwork = oe(nc.sbuf_tensor("mwork", [128, 2 * (T // 128)], F32))
        Wtm = oe(nc.sbuf_tensor("Wtm", [128, (T // 128) * E], BF16))

        semnames = ("s_xin s_cin s_rw s_wb0 s_wb1 s_wb2 s_wb3 s_gu0 s_gu1 s_wbfree s_x2 s_sqmm s_lnsum s_lnsq s_var s_std s_xn "
                    "s_qkvp s_evd s_eva s_gathp s_pS s_pBV s_Sm s_bv s_pO s_ob s_pOT s_ot "
                    "s_wop s_woadd s_lgp s_lgev s_lgtp s_lgtev s_negm s_exp s_wtm s_wtp s_wt "
                    "s_gufree s_dn s_dnfree s_pg s_pu s_sil s_hw s_py s_pyd s_wpe s_wdve s_out s_dbg "
                    "s_xt s_xta s_xtp s_Rp s_Rev s_rsel s_rtp s_rtev s_G s_xgp s_xge "
                    "s_wmm s_wrow s_wbb s_wc s_wsl s_xga s_gtbp s_gtd s_scp s_scadd s_ygev").split()
        SM = {n: oe(nc.semaphore(n)) for n in semnames}

        attn_scope = ExitStack()
        ae = attn_scope.enter_context
        WB = [ae(nc.sbuf_tensor(f"WB{i}", [128, CT * 128], BF16)) for i in range(4)]
        mean_bc = ae(nc.sbuf_tensor("mean_bc", [128, 512], F32))
        rstd_bc = ae(nc.sbuf_tensor("rstd_bc", [128, 512], F32))
        var_bc = ae(nc.sbuf_tensor("var_bc", [128, 512], F32))
        std_bc = ae(nc.sbuf_tensor("std_bc", [128, 512], F32))
        x2b = [ae(nc.sbuf_tensor(f"x2b{i}", [128, 512], BF16)) for i in range(2)]
        xcb = [ae(nc.sbuf_tensor(f"xcb{i}", [128, 512], BF16)) for i in range(2)]

        with nc.Block("b0") as b:
            @b.sync
            def _(sy):
                for ct in range(CT):
                    sy.dma_start(x1[:, ct * T:(ct + 1) * T], xT_v[ct]).then_inc(SM["s_xin"], 16)
                sy.dma_start(idb[:, :], c_idb.ap()).then_inc(SM["s_cin"], 16)
                sy.dma_start(idf[:, :], c_idf.ap()).then_inc(SM["s_cin"], 16)
                sy.dma_start(onesf[:, :], c_1f.ap()).then_inc(SM["s_cin"], 16)
                sy.dma_start(bdm[:, :], c_bd.ap()).then_inc(SM["s_cin"], 16)
                sy.dma_start(ltf[:, :], c_ltf.ap()).then_inc(SM["s_cin"], 16)
                sy.dma_start(iota[:, :], c_iota.ap()).then_inc(SM["s_cin"], 16)
                sy.dma_start(ioc[:, :], c_ioc.ap()).then_inc(SM["s_cin"], 16)
    
            @b.vector
            def _(v):
                v.memset(epsb[:, :], EPS)
                v.memset(onesb[:, :], 1.0)

            @b.gpsimd
            def _(g):
                g.dma_start(RW[:, :].rearrange("p (ct e) -> p ct e", e=E),
                            rwT_v).then_inc(SM["s_rw"], 16)
                for j in range(4):
                    g.dma_start(WB[j][:, :].rearrange("p (ct f) -> p ct f", f=128),
                                wk_v[:, :, j * 128:(j + 1) * 128]).then_inc(SM[f"s_wb{j}"], 16)

        C = dict(wb=4, wbfree=0, x2=0, sqmm=0, lnsum=0, lnsq=0, var=0, std=0, xn=0,
                 qkvp=0, evd=0, eva=0, gathp=0, pS=0, pBV=0, Sm=0, bv=0, pO=0, ob=0,
                 pOT=0, ot=0, wop=0, woadd=0)

        def emit_ln(b, ps_sum, ps_sq, th, NHl, woadd_base=None):
            base_x2 = C["x2"]; base_sq = C["sqmm"]
            base_sum = C["lnsum"]
            base_var = C["var"]; base_std = C["std"]

            @b.scalar
            def _(s):
                if base_x2 == 0:
                    s.wait_ge(SM["s_xin"], 16 * CT)
                for ct in range(CT):
                    if woadd_base is not None:
                        s.wait_ge(SM["s_woadd"], woadd_base + ct + 1)
                    if base_x2 + ct - 1 > 0:
                        s.wait_ge(SM["s_sqmm"], base_sq + ct - 1)
                    sl = slice(ct * T + th * NHl, ct * T + th * NHl + NHl)
                    s.activation(x2b[ct % 2][:, 0:NHl], x1[:, sl], AF.Square)
                    s.copy(xcb[ct % 2][:, 0:NHl], x1[:, sl]).then_inc(SM["s_x2"], 1)
                s.wait_ge(SM["s_var"], base_var + 1)
                s.activation(std_bc[:, 0:NHl], var_bc[:, 0:NHl], AF.Sqrt, bias=epsb[:, 0:1]).then_inc(SM["s_std"], 1)

            @b.tensor
            def _(t):
                if base_x2 == 0:
                    t.wait_ge(SM["s_xin"], 16 * CT)
                    t.wait_ge(SM["s_cin"], 16 * 7)
                if base_var >= 1:
                    t.wait_ge(SM["s_var"], base_var)
                for ct in range(CT):
                    t.wait_ge(SM["s_x2"], base_x2 + ct + 1)
                    i = t.matmul(ps_sum[:, 0:NHl], onesb[:, :], xcb[ct % 2][:, 0:NHl],
                                 start=(ct == 0), stop=(ct == CT - 1))
                    if ct == CT - 1:
                        i.then_inc(SM["s_lnsum"], 1)
                    i = t.matmul(ps_sq[:, 0:NHl], onesb[:, :], x2b[ct % 2][:, 0:NHl],
                                 start=(ct == 0), stop=(ct == CT - 1))
                    i.then_inc(SM["s_sqmm"], 1)

            @b.vector
            def _(v):
                v.wait_ge(SM["s_lnsum"], base_sum + 1)
                v.tensor_scalar(mean_bc[:, 0:NHl], ps_sum[:, 0:NHl], 1.0 / D, None, ALU.mult)
                v.wait_ge(SM["s_sqmm"], base_sq + CT)
                v.tensor_scalar(var_bc[:, 0:NHl], ps_sq[:, 0:NHl], 1.0 / D, None, ALU.mult)
                v.tensor_mul(rstd_bc[:, 0:NHl], mean_bc[:, 0:NHl], mean_bc[:, 0:NHl])
                v.tensor_sub(var_bc[:, 0:NHl], var_bc[:, 0:NHl], rstd_bc[:, 0:NHl]).then_inc(SM["s_var"], 1)
                v.wait_ge(SM["s_std"], base_std + 1)
                v.reciprocal(rstd_bc[:, 0:NHl], std_bc[:, 0:NHl])
                for ct in range(CT):
                    sl = slice(ct * T + th * NHl, ct * T + th * NHl + NHl)
                    v.tensor_sub(xsub[:, 0:NHl], x1[:, sl], mean_bc[:, 0:NHl])
                    v.tensor_mul(xn[:, sl], xsub[:, 0:NHl], rstd_bc[:, 0:NHl]).then_inc(SM["s_xn"], 1)

            C["x2"] += CT; C["sqmm"] += CT; C["lnsum"] += 1
            C["var"] += 1; C["std"] += 1; C["xn"] += CT

        if True:
            QT = ae(nc.sbuf_tensor("QT", [128, H * NH], BF16))
            KT = ae(nc.sbuf_tensor("KT", [128, H * NH], BF16))
            VT = ae(nc.sbuf_tensor("VT", [128, H * NH], BF16))
            OT = ae(nc.sbuf_tensor("OT", [128, H * NH], BF16))
            aQ = [ae(nc.sbuf_tensor(f"aQ{i}", [128, 128], BF16)) for i in range(2)]
            aK = [ae(nc.sbuf_tensor(f"aK{i}", [128, 128], BF16)) for i in range(2)]
            aV = [ae(nc.sbuf_tensor(f"aV{i}", [128, 128], BF16)) for i in range(2)]
            Smb = [ae(nc.sbuf_tensor(f"Smb{i}", [128, 128], BF16)) for i in range(2)]
            bvb = [ae(nc.sbuf_tensor(f"bvb{i}", [128, 128], BF16)) for i in range(2)]
            obb = [ae(nc.sbuf_tensor(f"obb{i}", [128, 128], BF16)) for i in range(2)]

            for th in range(2):
                wviews = [wk_v, wv_v, wq_v]
                dsts = [KT, VT, QT]

                with nc.psum_tensor(f"ps_sum{th}", [128, NH], F32) as ps_sum, \
                     nc.psum_tensor(f"ps_sq{th}", [128, NH], F32) as ps_sq, \
                     nc.psum_tensor(f"ps_k0_{th}", [128, NH], F32) as ps_k0, \
                     nc.psum_tensor(f"ps_k1_{th}", [128, NH], F32) as ps_k1, \
                     nc.Block(f"qkv{th}") as b:
                    ps_k = [ps_k0, ps_k1]
                    if th == 0:
                        emit_ln(b, ps_sum, ps_sq, 0, NH)
                    base_wb = C["wb"]; base_wbfree = C["wbfree"]
                    base_qkvp = C["qkvp"]; base_evd = C["evd"]; base_eva = C["eva"]
                    base_xn = th * CT

                    @b.gpsimd
                    def _(g, th=th):
                        for m in range(48):
                            gidx = th * 64 + m
                            if th == 0 and m < 4:
                                continue
                            w = wviews[m // 16]
                            j = (m % 16) * 128
                            if m >= 4:
                                g.wait_ge(SM["s_qkvp"], th * 48 + m - 3)
                            g.dma_start(WB[gidx % 4][:, :].rearrange("p (ct f) -> p ct f", f=128),
                                        w[:, :, j:j + 128]).then_inc(SM[f"s_wb{gidx % 4}"], 16)

                    @b.tensor
                    def _(t, th=th):
                        for m in range(48):
                            gidx = th * 64 + m
                            t.wait_ge(SM[f"s_wb{gidx % 4}"], 16 * (gidx // 4 + 1))
                            par = m % 2
                            if m >= 2:
                                if par == 0:
                                    t.wait_ge(SM["s_evd"], base_evd + m // 2)
                                else:
                                    t.wait_ge(SM["s_eva"], base_eva + m // 2)
                            for ct in range(CT):
                                if m == 0:
                                    t.wait_ge(SM["s_xn"], base_xn + ct + 1)
                                i = t.matmul(ps_k[par][:, 0:NH],
                                             WB[gidx % 4][:, ct * 128:(ct + 1) * 128],
                                             xn[:, ct * T + th * NH: ct * T + th * NH + NH],
                                             start=(ct == 0), stop=(ct == CT - 1))
                            i.then_inc(SM["s_qkvp"], 1)

                    @b.vector
                    def _(v, th=th):
                        for m in range(0, 48, 2):
                            v.wait_ge(SM["s_qkvp"], base_qkvp + m + 1)
                            dst = dsts[m // 16]
                            jt = m % 16
                            v.tensor_copy(dst[:, jt * NH:(jt + 1) * NH], ps_k[0][:, 0:NH]).then_inc(SM["s_evd"], 1)

                    @b.scalar
                    def _(s, th=th):
                        for m in range(1, 48, 2):
                            s.wait_ge(SM["s_qkvp"], base_qkvp + m + 1)
                            dst = dsts[m // 16]
                            jt = m % 16
                            s.copy(dst[:, jt * NH:(jt + 1) * NH], ps_k[1][:, 0:NH]).then_inc(SM["s_eva"], 1)

                    if th == 0:
                        emit_ln(b, ps_sum, ps_sq, 1, NH)
                    C["qkvp"] += 48; C["evd"] += 24; C["eva"] += 24

                with nc.psum_tensor(f"ps_S{th}_0", [128, 128], F32) as ps_S0, \
                     nc.psum_tensor(f"ps_S{th}_1", [128, 128], F32) as ps_S1, \
                     nc.psum_tensor(f"ps_BV{th}_0", [128, 128], BF16) as ps_BV0, \
                     nc.psum_tensor(f"ps_BV{th}_1", [128, 128], BF16) as ps_BV1, \
                     nc.psum_tensor(f"ps_O{th}_0", [128, 128], F32) as ps_O0, \
                     nc.psum_tensor(f"ps_O{th}_1", [128, 128], F32) as ps_O1, \
                     nc.psum_tensor(f"ps_OT{th}_0", [128, 128], BF16) as ps_OT0, \
                     nc.psum_tensor(f"ps_OT{th}_1", [128, 128], BF16) as ps_OT1, \
                     nc.Block(f"eins{th}") as b:
                    ps_S = [ps_S0, ps_S1]; ps_BV = [ps_BV0, ps_BV1]
                    ps_O = [ps_O0, ps_O1]; ps_OT = [ps_OT0, ps_OT1]
                    bg = dict(C)

                    QT3 = QT[:, :].rearrange("p (h t) -> p h t", t=NH)
                    KT3 = KT[:, :].rearrange("p (h t) -> p h t", t=NH)
                    VT3 = VT[:, :].rearrange("p (h t) -> p h t", t=NH)
                    OT3 = OT[:, :].rearrange("p (h t) -> p h t", t=NH)

                    if dbg and th == 0:
                        @b.sync
                        def _(sy):
                            sy.dma_start(d_xn.ap(), xn[:, :]).then_inc(SM["s_dbg"], 16)
                            sy.dma_start(d_qt.ap(), QT[:, :]).then_inc(SM["s_dbg"], 16)
                            sy.dma_start(d_kt.ap(), KT[:, :]).then_inc(SM["s_dbg"], 16)
                            sy.dma_start(d_vt.ap(), VT[:, :]).then_inc(SM["s_dbg"], 16)

                    @b.gpsimd
                    def _(g, th=th):
                        for k in range(4):
                            gidx = th * 64 + 48 + k
                            g.dma_start(WB[gidx % 4][:, :].rearrange("p (ct f) -> p ct f", f=128),
                                        wo_v[:, :, k * 128:(k + 1) * 128]).then_inc(SM[f"s_wb{gidx % 4}"], 16)
                        for gi in range(G):
                            p = gi % 2
                            if gi >= 2:
                                g.wait_ge(SM["s_pS"], bg["pS"] + gi - 1)
                                g.wait_ge(SM["s_pBV"], bg["pBV"] + gi - 1)
                            sl = slice(gi * 8, gi * 8 + 8)
                            g.tensor_copy(aQ[p][:, :].rearrange("p (h t) -> p h t", t=8), QT3[:, :, sl])
                            g.tensor_copy(aK[p][:, :].rearrange("p (h t) -> p h t", t=8), KT3[:, :, sl])
                            g.tensor_copy(aV[p][:, :].rearrange("p (h t) -> p h t", t=8), VT3[:, :, sl]).then_inc(SM["s_gathp"], 1)

                    @b.tensor
                    def _(t, th=th):
                        for gi in range(G + 2):
                            if gi < G:
                                p = gi % 2
                                t.wait_ge(SM["s_gathp"], bg["gathp"] + gi + 1)
                                if gi >= 2:
                                    t.wait_ge(SM["s_Sm"], bg["Sm"] + gi - 1)
                                t.matmul(ps_S[p][:, :], aK[p][:, :], aQ[p][:, :],
                                         start=True, stop=True).then_inc(SM["s_pS"], 1)
                                if gi >= 2:
                                    t.wait_ge(SM["s_bv"], bg["bv"] + gi - 1)
                                t.transpose(ps_BV[p][:, :], aV[p][:, :], idb[:, :]).then_inc(SM["s_pBV"], 1)
                            jt = gi - 2
                            if 0 <= jt < G:
                                q = jt % 2
                                t.wait_ge(SM["s_ob"], bg["ob"] + jt + 1)
                                if jt >= 2:
                                    t.wait_ge(SM["s_ot"], bg["ot"] + jt - 1)
                                t.transpose(ps_OT[q][:, :], obb[q][:, :], idb[:, :]).then_inc(SM["s_pOT"], 1)
                            j = gi - 1
                            if 0 <= j < G:
                                q = j % 2
                                t.wait_ge(SM["s_Sm"], bg["Sm"] + j + 1)
                                t.wait_ge(SM["s_bv"], bg["bv"] + j + 1)
                                if j >= 2:
                                    t.wait_ge(SM["s_ob"], bg["ob"] + j - 1)
                                t.matmul(ps_O[q][:, :], Smb[q][:, :], bvb[q][:, :],
                                         start=True, stop=True).then_inc(SM["s_pO"], 1)

                    @b.vector
                    def _(v, th=th):
                        for gi in range(G):
                            p = gi % 2
                            v.wait_ge(SM["s_pS"], bg["pS"] + gi + 1)
                            if gi >= 2:
                                v.wait_ge(SM["s_pO"], bg["pO"] + gi - 1)
                            v.tensor_mul(Smb[p][:, :], ps_S[p][:, :], bdm[:, :]).then_inc(SM["s_Sm"], 1)
                            v.wait_ge(SM["s_pOT"], bg["pOT"] + gi + 1)
                            v.tensor_copy(OT3[:, :, gi * 8:gi * 8 + 8],
                                          ps_OT[p][:, :].rearrange("p (h t) -> p h t", t=8)).then_inc(SM["s_ot"], 1)

                    @b.scalar
                    def _(s, th=th):
                        for gi in range(G):
                            p = gi % 2
                            s.wait_ge(SM["s_pBV"], bg["pBV"] + gi + 1)
                            if gi >= 2:
                                s.wait_ge(SM["s_pO"], bg["pO"] + gi - 1)
                            s.copy(bvb[p][:, :], ps_BV[p][:, :]).then_inc(SM["s_bv"], 1)
                            s.wait_ge(SM["s_pO"], bg["pO"] + gi + 1)
                            if gi >= 2:
                                s.wait_ge(SM["s_pOT"], bg["pOT"] + gi - 1)
                            s.copy(obb[p][:, :], ps_O[p][:, :]).then_inc(SM["s_ob"], 1)

                    C["wb"] += 4
                    C["gathp"] += G; C["pS"] += G; C["pBV"] += G; C["Sm"] += G
                    C["bv"] += G; C["pO"] += G; C["ob"] += G; C["pOT"] += G; C["ot"] += G

                with nc.psum_tensor(f"ps_wo{th}_0", [128, NH], F32) as ps_wo0, \
                     nc.psum_tensor(f"ps_wo{th}_1", [128, NH], F32) as ps_wo1, \
                     nc.psum_tensor(f"ps2_sum{th}", [128, NH], F32) as ps_sum2, \
                     nc.psum_tensor(f"ps2_sq{th}", [128, NH], F32) as ps_sq2, \
                     nc.Block(f"wo{th}") as b:
                    ps_wo = [ps_wo0, ps_wo1]
                    base_wop = C["wop"]; base_woadd = C["woadd"]

                    if dbg and th == 0:
                        @b.sync
                        def _(sy):
                            sy.dma_start(d_ot.ap(), OT[:, :]).then_inc(SM["s_dbg"], 16)

                    @b.gpsimd
                    def _(g, th=th):
                        for k in range(4, 16):
                            gidx = th * 64 + 48 + k
                            g.wait_ge(SM["s_wop"], th * 16 + k - 3)
                            g.dma_start(WB[gidx % 4][:, :].rearrange("p (ct f) -> p ct f", f=128),
                                        wo_v[:, :, k * 128:(k + 1) * 128]).then_inc(SM[f"s_wb{gidx % 4}"], 16)

                    @b.tensor
                    def _(t, th=th):
                        for jt in range(16):
                            gidx = th * 64 + 48 + jt
                            t.wait_ge(SM[f"s_wb{gidx % 4}"], 16 * (gidx // 4 + 1))
                            par = jt % 2
                            if jt >= 2:
                                t.wait_ge(SM["s_woadd"], base_woadd + jt - 1)
                            for ct in range(CT):
                                i = t.matmul(ps_wo[par][:, 0:NH],
                                             WB[gidx % 4][:, ct * 128:(ct + 1) * 128],
                                             OT[:, ct * NH:(ct + 1) * NH],
                                             start=(ct == 0), stop=(ct == CT - 1))
                            i.then_inc(SM["s_wop"], 1)

                    @b.vector
                    def _(v, th=th):
                        for jt in range(16):
                            par = jt % 2
                            v.wait_ge(SM["s_wop"], base_wop + jt + 1)
                            sl = slice(jt * T + th * NH, jt * T + th * NH + NH)
                            v.tensor_tensor(x1[:, sl], x1[:, sl], ps_wo[par][:, 0:NH],
                                            ALU.add).then_inc(SM["s_woadd"], 1)

                    emit_ln(b, ps_sum2, ps_sq2, th, NH, woadd_base=base_woadd)
                    C["wop"] += 16; C["woadd"] += 16

        attn_scope.close()

        NHl = NH
        TT = T // 128
        with nc.psum_tensor("ps_lg", [E, NH], F32) as ps_lg, \
             nc.psum_tensor("ps_lgt", [128, E], F32) as ps_lgt, \
             nc.Block("router") as b:
            br = dict(C)
            base_xn2 = C["xn"] - 2 * CT

            @b.tensor
            def _(t):
                t.wait_ge(SM["s_rw"], 16)
                for thl in range(2):
                    t.wait_ge(SM["s_xn"], base_xn2 + CT * (thl + 1))
                    for ct in range(CT):
                        i = t.matmul(ps_lg[:, 0:NH], RW[:, ct * E:(ct + 1) * E],
                                     xn[:, ct * T + thl * NH: ct * T + thl * NH + NH],
                                     start=(ct == 0), stop=(ct == CT - 1))
                    i.then_inc(SM["s_lgp"], 1)
                    t.wait_ge(SM["s_lgev"], thl + 1)
                for tt in range(TT):
                    if tt >= 1:
                        t.wait_ge(SM["s_lgtev"], tt)
                    t.transpose(ps_lgt[:, :], lg_sb[:, tt * 128:(tt + 1) * 128],
                                idf[0:E, 0:E]).then_inc(SM["s_lgtp"], 1)

            @b.vector
            def _(v):
                def spacer():
                    v.tensor_copy(xsub[:, 0:256], x1[:, 0:256])

                def bcast(ap2d):
                    return ap2d.unsqueeze(2).broadcast_to((128, TT, E))

                lgt3 = lgt[:, :].rearrange("p (tt e) -> p tt e", e=E)
                ework3 = ework[:, :].rearrange("p (tt e) -> p tt e", e=E)
                wselb = hitb
                wselb3 = wselb[:, :].rearrange("p (tt e) -> p tt e", e=E)
                for thl in range(2):
                    v.wait_ge(SM["s_lgp"], thl + 1)
                    v.tensor_copy(lg_sb[:, thl * NH:(thl + 1) * NH], ps_lg[:, 0:NH]).then_inc(SM["s_lgev"], 1)
                v.memset(selm[:, :], 0.0)
                for tt in range(TT):
                    v.wait_ge(SM["s_lgtp"], tt + 1)
                    v.tensor_copy(lgt[:, tt * E:(tt + 1) * E], ps_lgt[:, :]).then_inc(SM["s_lgtev"], 1)
                spacer()
                v.reduce_max(mneg[:, 0:TT], lgt3, axis=AX.X, negate=True)
                spacer()
                v.tensor_tensor(ework3, lgt3, bcast(mneg[:, 0:TT]), ALU.add).then_inc(SM["s_negm"], 1)
                v.wait_ge(SM["s_exp"], 1)
                v.tensor_copy(ework[:, :], ew[:, :])
                for r in range(TOPK):
                    v.reduce_max(mwork[:, 0:TT], ework3, axis=AX.X)
                    spacer()
                    v.tensor_tensor(hitb[:, :].rearrange("p (tt e) -> p tt e", e=E),
                                    ework3, bcast(mwork[:, 0:TT]), ALU.is_ge)
                    v.tensor_add(selm[:, :], selm[:, :], hitb[:, :])
                    v.scalar_tensor_tensor(ework[:, :], hitb[:, :], -BIG, ework[:, :],
                                           ALU.mult, ALU.add)
                v.tensor_mul(wselb[:, :], ew[:, :], selm[:, :])
                v.reduce_sum(mwork[:, TT:2 * TT], wselb3, axis=AX.X)
                spacer()
                v.reciprocal(mwork[:, 0:TT], mwork[:, TT:2 * TT])
                spacer()
                v.tensor_tensor(Wtm[:, :].rearrange("p (tt e) -> p tt e", e=E),
                                wselb3, bcast(mwork[:, 0:TT]), ALU.mult).then_inc(SM["s_wtm"], 1)

            @b.scalar
            def _(s):
                s.wait_ge(SM["s_negm"], 1)
                s.activation(ew[:, :], ework[:, :], AF.Exp).then_inc(SM["s_exp"], 1)


        NS = min(512, T)
        NTC = T // NS
        with ExitStack() as moe_scope:
            me = moe_scope.enter_context
            GT = [me(nc.sbuf_tensor(f"GT{i}", [128, CT * 128], BF16)) for i in range(2)]
            UT = [me(nc.sbuf_tensor(f"UT{i}", [128, CT * 128], BF16)) for i in range(2)]
            DT = me(nc.sbuf_tensor("DT", [128, ITn * D], BF16))
            Xg = me(nc.sbuf_tensor("Xg", [128, CT * CAP], BF16))
            hg = me(nc.sbuf_tensor("hg", [128, ITn * CAP], BF16))
            Gb = [me(nc.sbuf_tensor(f"Gb{i}", [128, TT * 128], BF16)) for i in range(2)]
            GTb = [[me(nc.sbuf_tensor(f"GTb{i}_{s2}", [128, T // 2], BF16)) for s2 in range(2)] for i in range(2)]
            Ygb = [[me(nc.sbuf_tensor(f"Ygb{i}_{s2}", [128, D], BF16)) for s2 in range(2)] for i in range(2)]
            wcol = me(nc.sbuf_tensor("wcol", [128, 4], F32))
            Rself = me(nc.sbuf_tensor("Rself", [128, TT * E], F32))
            RT = me(nc.sbuf_tensor("RT", [E, T], BF16))
            wrow = me(nc.sbuf_tensor("wrow", [1, CAP], BF16))

            with nc.psum_tensor("ps_t0", [128, 128], BF16) as ps_t0, \
                 nc.psum_tensor("ps_t1", [128, 128], BF16) as ps_t1, \
                 nc.psum_tensor("ps_R0", [128, E], F32) as ps_R0, \
                 nc.psum_tensor("ps_R1", [128, E], F32) as ps_R1, \
                 nc.psum_tensor("ps_rt0", [E, 128], F32) as ps_rt0, \
                 nc.psum_tensor("ps_rt1", [E, 128], F32) as ps_rt1, \
                 nc.Block("sprep") as b:
                ps_t = [ps_t0, ps_t1]
                ps_R = [ps_R0, ps_R1]
                ps_rt = [ps_rt0, ps_rt1]

                if dbg:
                    @b.sync
                    def _(sy):
                        sy.dma_start(d_x1a.ap(), x1[:, :]).then_inc(SM["s_dbg"], 16)
                        sy.dma_start(d_lgt.ap(), lgt[:, :]).then_inc(SM["s_dbg"], 16)
                        sy.dma_start(d_ew.ap(), ew[:, :]).then_inc(SM["s_dbg"], 16)
                        sy.dma_start(d_selm.ap(), selm[:, :]).then_inc(SM["s_dbg"], 16)
                        sy.dma_start(d_mneg.ap(), mneg[:, :]).then_inc(SM["s_dbg"], 16)
                        sy.dma_start(d_wtm.ap(), Wtm[:, :]).then_inc(SM["s_dbg"], 16)

                @b.gpsimd
                def _(g):
                    for k in range(2):
                        g.dma_start(GT[k][:, :].rearrange("p (ct f) -> p ct f", f=128),
                                    gT_v[0][:, :, k * 128:(k + 1) * 128]).then_inc(SM[f"s_gu{k}"], 16)
                        g.dma_start(UT[k][:, :].rearrange("p (ct f) -> p ct f", f=128),
                                    uT_v[0][:, :, k * 128:(k + 1) * 128]).then_inc(SM[f"s_gu{k}"], 16)
                    g.dma_start(DT[:, :].rearrange("p (it f) -> p it f", f=D),
                                dT_v[0]).then_inc(SM["s_dn"], 16)

                @b.tensor
                def _(t):
                    t.wait_ge(SM["s_cin"], 16 * 7)
                    idx = 0
                    for tt in range(TT):
                        for ct in range(CT):
                            par = idx % 2
                            if idx >= 2:
                                if par == 0:
                                    t.wait_ge(SM["s_xt"], idx // 2)
                                else:
                                    t.wait_ge(SM["s_xta"], idx // 2)
                            t.transpose(ps_t[par][:, :],
                                        xn[:, ct * T + tt * 128: ct * T + tt * 128 + 128],
                                        idb[:, :]).then_inc(SM["s_xtp"], 1)
                            idx += 1
                    for tt in range(TT):
                        par = tt % 2
                        hstart = 0 if tt < TT // 2 else TT // 2
                        if tt >= 2:
                            t.wait_ge(SM["s_Rev"], tt - 1)
                        i = t.matmul(ps_R[par][:, :], ltf[:, :], selm[:, tt * E:(tt + 1) * E],
                                     start=True, stop=(tt == hstart))
                        for tp in range(hstart, tt):
                            i = t.matmul(ps_R[par][:, :], onesf[:, :], selm[:, tp * E:(tp + 1) * E],
                                         start=False, stop=(tp == tt - 1))
                        i.then_inc(SM["s_Rp"], 1)
                    for tt in range(TT):
                        par = tt % 2
                        t.wait_ge(SM["s_rsel"], 1)
                        if tt >= 2:
                            t.wait_ge(SM["s_rtev"], tt - 1)
                        t.transpose(ps_rt[par][:, :], Rself[:, tt * E:(tt + 1) * E],
                                    idf[:, :]).then_inc(SM["s_rtp"], 1)

                @b.vector
                def _(v):
                    v.wait_ge(SM["s_cin"], 16 * 7)
                    idx = 0
                    for tt in range(TT):
                        for ct in range(CT):
                            if idx % 2 == 0:
                                v.wait_ge(SM["s_xtp"], idx + 1)
                                v.tensor_copy(xn[:, ct * T + tt * 128: ct * T + tt * 128 + 128],
                                              ps_t[0][:, :]).then_inc(SM["s_xt"], 1)
                            idx += 1
                    for tt in range(TT):
                        v.wait_ge(SM["s_Rp"], tt + 1)
                        v.tensor_copy(Rself[:, tt * E:(tt + 1) * E], ps_R[tt % 2][:, :]).then_inc(SM["s_Rev"], 1)
                    v.tensor_copy(xsub[:, 0:256], x1[:, 0:256])
                    v.tensor_scalar(Rself[:, :], Rself[:, :], BIGS, None, ALU.add)
                    i = v.scalar_tensor_tensor(Rself[:, :], selm[:, :], -BIGS, Rself[:, :],
                                               ALU.mult, ALU.add)
                    i.then_inc(SM["s_rsel"], 1)
                    for tt in range(TT):
                        v.wait_ge(SM["s_rtp"], tt + 1)
                        v.tensor_copy(RT[0:E, tt * 128:(tt + 1) * 128], ps_rt[tt % 2][:, :]).then_inc(SM["s_rtev"], 1)
                    for tt in range(TT):
                        i = v.tensor_single_scalar(Gb[0][:, tt * 128:(tt + 1) * 128], iota[:, 0:128],
                                                   Rself[:, (tt * E):(tt * E) + 1], ALU.is_equal)
                    i.then_inc(SM["s_G"], 1)

                @b.scalar
                def _(s):
                    idx = 0
                    for tt in range(TT):
                        for ct in range(CT):
                            if idx % 2 == 1:
                                s.wait_ge(SM["s_xtp"], idx + 1)
                                s.copy(xn[:, ct * T + tt * 128: ct * T + tt * 128 + 128],
                                       ps_t[1][:, :]).then_inc(SM["s_xta"], 1)
                            idx += 1

            with nc.psum_tensor("A0", [128, 512], F32) as A0, \
                 nc.psum_tensor("A1", [128, 512], F32) as A1, \
                 nc.psum_tensor("A2", [128, 512], F32) as A2, \
                 nc.psum_tensor("A3", [128, 512], F32) as A3, \
                 nc.psum_tensor("A4", [128, 512], F32) as A4, \
                 nc.psum_tensor("A5", [128, 512], F32) as A5, \
                 nc.psum_tensor("A6", [128, 512], F32) as A6, \
                 nc.psum_tensor("A7", [128, 512], F32) as A7, \
                 nc.Block("smoe") as b:
                ps_gs = [A0, A1]
                ps_g = [A2, A3]
                ps_u = [A4, A5]
                ps_w = A6
                ps_gtb = A7
                SCN = 16 * NTC

                @b.gpsimd
                def _(g):
                    for e in range(E):
                        for c in range(CT0G):
                            k = CT0G * e + c
                            if k < 2:
                                continue
                            g.wait_ge(SM["s_pu"], k - 1)
                            g.dma_start(GT[k % 2][:, :].rearrange("p (ct f) -> p ct f", f=128),
                                        gT_v[e][:, :, c * 128:(c + 1) * 128]).then_inc(SM[f"s_gu{k % 2}"], 16)
                            g.dma_start(UT[k % 2][:, :].rearrange("p (ct f) -> p ct f", f=128),
                                        uT_v[e][:, :, c * 128:(c + 1) * 128]).then_inc(SM[f"s_gu{k % 2}"], 16)
                            if c == 0 and e >= 1:
                                g.wait_ge(SM["s_py"], 8 * e)
                                g.dma_start(DT[:, :].rearrange("p (it f) -> p it f", f=D),
                                            dT_v[e]).then_inc(SM["s_dn"], 16)

                @b.tensor
                def _(t):
                    t.wait_ge(SM["s_rtev"], TT)
                    t.wait_ge(SM["s_G"], 1)
                    t.wait_ge(SM["s_wtm"], 1)
                    for tc in range(NTC):
                        if tc >= 1:
                            t.wait_ge(SM["s_gtd"], tc)
                        t.matmul(ps_gtb[:, 0:NS], idb[0:E, 0:1].broadcast_to((E, 128)),
                                 RT[0:E, tc * NS:(tc + 1) * NS],
                                 start=True, stop=True).then_inc(SM["s_gtbp"], 1)
                    for tt in range(TT):
                        half = 0 if tt < TT // 2 else 1
                        i = t.matmul(ps_w[0:1, 256 + half * 128:256 + half * 128 + 128],
                                     Wtm[:, 0:1],
                                     Gb[0][:, tt * 128:(tt + 1) * 128],
                                     start=(tt == 0), stop=(tt == TT - 1))
                    i.then_inc(SM["s_wmm"], 1)
                    t.wait_ge(SM["s_wrow"], 1)
                    for sblk in range(2):
                        t.matmul(ps_w[:, sblk:sblk + 1],
                                 wrow[0:1, sblk * 128:(sblk + 1) * 128],
                                 onesb[0:1, 0:1],
                                 start=True, stop=True).then_inc(SM["s_wc"], 1)

                    for e in range(E):
                        q = e // 2
                        for ct in range(CT):
                            par = ct % 2
                            gi = 16 * e + ct
                            if e % 2 == 0:
                                if e >= 2 and ct < 2:
                                    t.wait_ge(SM["s_scadd"], SCN * q)
                            if gi - 2 >= 0 and not (e % 2 == 0 and ct < 2):
                                g2 = gi - 2
                                if g2 % 2 == 0:
                                    t.wait_ge(SM["s_xge"], g2 // 2 + 1)
                                else:
                                    t.wait_ge(SM["s_xga"], g2 // 2 + 1)
                            for tt in range(TT):
                                half = 0 if tt < TT // 2 else 1
                                i = t.matmul(ps_gs[par][:, half * 128:(half + 1) * 128],
                                             xn[:, ct * T + tt * 128: ct * T + tt * 128 + 128],
                                             Gb[e % 2][:, tt * 128:(tt + 1) * 128],
                                             start=(tt == 0), stop=(tt == TT - 1))
                            i.then_inc(SM["s_xgp"], 1)
                        if e + 1 < E:
                            t.wait_ge(SM["s_G"], e + 2)
                            for tc in range(NTC):
                                t.wait_ge(SM["s_gtd"], NTC * (e + 1) + tc)
                                t.matmul(ps_gtb[:, 0:NS],
                                         idb[0:E, (e + 1):(e + 2)].broadcast_to((E, 128)),
                                         RT[0:E, tc * NS:(tc + 1) * NS],
                                         start=True, stop=True).then_inc(SM["s_gtbp"], 1)
                            t.wait_ge(SM["s_wsl"], e + 1)
                            for tt in range(TT):
                                half = 0 if tt < TT // 2 else 1
                                i = t.matmul(ps_w[0:1, 256 + half * 128:256 + half * 128 + 128],
                                             Wtm[:, (tt * E + e + 1):(tt * E + e + 1) + 1],
                                             Gb[(e + 1) % 2][:, tt * 128:(tt + 1) * 128],
                                             start=(tt == 0), stop=(tt == TT - 1))
                            i.then_inc(SM["s_wmm"], 1)
                        for c in range(CT0G):
                            k = CT0G * e + c
                            t.wait_ge(SM[f"s_gu{k % 2}"], 16 * (k // 2 + 1))
                            m = k
                            par = m % 2
                            if m >= 2:
                                t.wait_ge(SM["s_hw"], m - 1)
                            if c == 0:
                                t.wait_ge(SM["s_xge"], 8 * (e + 1))
                                t.wait_ge(SM["s_xga"], 8 * (e + 1))
                            for ct in range(CT):
                                i = t.matmul(ps_g[par][:, 0:CAP],
                                             GT[k % 2][:, ct * 128:(ct + 1) * 128],
                                             Xg[:, ct * CAP:(ct + 1) * CAP],
                                             start=(ct == 0), stop=(ct == CT - 1))
                            i.then_inc(SM["s_pg"], 1)
                            if e >= 1 and c < 2:
                                t.wait_ge(SM["s_ygev"], 8 * e)
                            for ct in range(CT):
                                i = t.matmul(ps_u[par][:, 0:CAP],
                                             UT[k % 2][:, ct * 128:(ct + 1) * 128],
                                             Xg[:, ct * CAP:(ct + 1) * CAP],
                                             start=(ct == 0), stop=(ct == CT - 1))
                            i.then_inc(SM["s_pu"], 1)
                            if c == 2 and e + 1 < E:
                                t.wait_ge(SM["s_wrow"], e + 2)
                                for sblk in range(2):
                                    t.matmul(ps_w[:, sblk:sblk + 1],
                                             wrow[0:1, sblk * 128:(sblk + 1) * 128],
                                             onesb[0:1, 0:1],
                                             start=True, stop=True).then_inc(SM["s_wc"], 1)
                        t.wait_ge(SM["s_dn"], 16 * (e + 1))
                        t.wait_ge(SM["s_hw"], CT0G * (e + 1))
                        for f in range(8):
                            sblk, hc = f // 4, f % 4
                            par = f % 2
                            fy = 8 * e + f
                            if fy >= 2:
                                t.wait_ge(SM["s_ygev"], fy - 1)
                            for it in range(ITn):
                                i = t.matmul(ps_u[par][:, 0:512],
                                             hg[:, it * CAP + sblk * 128: it * CAP + sblk * 128 + 128],
                                             DT[:, it * D + hc * 512: it * D + hc * 512 + 512],
                                             start=(it == 0), stop=(it == ITn - 1))
                            i.then_inc(SM["s_py"], 1)
                        if e % 2 == 1:
                            t.wait_ge(SM["s_ygev"], 8 * (e + 1))
                            t.wait_ge(SM["s_gtd"], NTC * (e + 1))
                            for j in range(SCN):
                                hct, tc = j // NTC, j % NTC
                                par = j % 2
                                sj = SCN * q + j
                                if j < 2:
                                    lastg = 16 * (e + 1) - 1
                                    t.wait_ge(SM["s_xge"], (lastg - 1) // 2 + 1)
                                    t.wait_ge(SM["s_xga"], lastg // 2 + 1)
                                if sj >= 2:
                                    t.wait_ge(SM["s_scadd"], sj - 1)
                                HLEN = T // 2
                                parts = [(h, max(tc * NS, h * HLEN) - tc * NS,
                                          min((tc + 1) * NS, (h + 1) * HLEN) - max(tc * NS, h * HLEN))
                                         for h in range(2)
                                         if max(tc * NS, h * HLEN) < min((tc + 1) * NS, (h + 1) * HLEN)]
                                first = True
                                for ep in range(2):
                                    for (h, off, ln) in parts:
                                        hb = tc * NS + off - h * HLEN
                                        i = t.matmul(ps_gs[par][:, off:off + ln],
                                                     Ygb[ep][h][:, hct * 128:(hct + 1) * 128],
                                                     GTb[ep][h][:, hb:hb + ln],
                                                     start=first,
                                                     stop=(ep == 1 and (h, off, ln) == parts[-1]))
                                        first = False
                                i.then_inc(SM["s_scp"], 1)

                @b.scalar
                def _(s):
                    for e in range(E):
                        for ct in range(1, CT, 2):
                            gi = 16 * e + ct
                            s.wait_ge(SM["s_xgp"], gi + 1)
                            s.copy(Xg[:, ct * CAP:(ct + 1) * CAP],
                                   ps_gs[ct % 2][:, 0:CAP]).then_inc(SM["s_xga"], 1)
                        for c in range(CT0G):
                            m = CT0G * e + c
                            par = m % 2
                            s.wait_ge(SM["s_pg"], m + 1)
                            s.activation(hg[:, c * CAP:(c + 1) * CAP], ps_g[par][:, 0:CAP],
                                         AF.Silu).then_inc(SM["s_sil"], 1)
                        s.wait_ge(SM["s_wsl"], e + 1)
                        if e >= 2:
                            s.wait_ge(SM["s_scp"], SCN * (e // 2))
                        for f in range(8):
                            sblk, hc = f // 4, f % 4
                            s.wait_ge(SM["s_py"], 8 * e + f + 1)
                            s.activation(Ygb[e % 2][sblk][:, hc * 512:(hc + 1) * 512],
                                         ps_u[f % 2][:, 0:512], AF.Copy,
                                         scale=wcol[:, (e % 2) * 2 + sblk:(e % 2) * 2 + sblk + 1]).then_inc(SM["s_ygev"], 1)

                @b.vector
                def _(v):
                    HLEN = T // 2
                    def hgrp(tc):
                        lo, hi = tc * NS, (tc + 1) * NS
                        return [(h, max(lo, h * HLEN) - lo, min(hi, (h + 1) * HLEN) - max(lo, h * HLEN))
                                for h in range(2) if max(lo, h * HLEN) < min(hi, (h + 1) * HLEN)]
                    for tc in range(NTC):
                        v.wait_ge(SM["s_gtbp"], tc + 1)
                        for (h, off, ln) in hgrp(tc):
                            hb = tc * NS + off - h * HLEN
                            i = v.tensor_single_scalar(GTb[0][h][:, hb:hb + ln],
                                                       ps_gtb[:, off:off + ln],
                                                       ioc[:, 0:1], ALU.is_equal)
                        i.then_inc(SM["s_gtd"], 1)
                    v.wait_ge(SM["s_wmm"], 1)
                    v.tensor_copy(wrow[0:1, :], ps_w[0:1, 256:256 + CAP]).then_inc(SM["s_wrow"], 1)
                    v.wait_ge(SM["s_wc"], 2)
                    v.tensor_copy(wcol[:, 0:2], ps_w[:, 0:2]).then_inc(SM["s_wsl"], 1)
                    for e in range(E):
                        q = e // 2
                        if e + 1 < E:
                            if e >= 1:
                                v.wait_ge(SM["s_xgp"], 16 * e)
                            for tt in range(TT):
                                i = v.tensor_single_scalar(Gb[(e + 1) % 2][:, tt * 128:(tt + 1) * 128],
                                                           iota[:, 0:128],
                                                           Rself[:, (tt * E + e + 1):(tt * E + e + 1) + 1],
                                                           ALU.is_equal)
                            i.then_inc(SM["s_G"], 1)
                        for ct in range(0, CT, 2):
                            gi = 16 * e + ct
                            v.wait_ge(SM["s_xgp"], gi + 1)
                            v.tensor_copy(Xg[:, ct * CAP:(ct + 1) * CAP],
                                          ps_gs[ct % 2][:, 0:CAP]).then_inc(SM["s_xge"], 1)
                        if e + 1 < E:
                            for tc in range(NTC):
                                v.wait_ge(SM["s_gtbp"], NTC * (e + 1) + tc + 1)
                                for (h, off, ln) in hgrp(tc):
                                    hb = tc * NS + off - h * HLEN
                                    i = v.tensor_single_scalar(GTb[(e + 1) % 2][h][:, hb:hb + ln],
                                                               ps_gtb[:, off:off + ln],
                                                               ioc[:, 0:1], ALU.is_equal)
                                i.then_inc(SM["s_gtd"], 1)
                            v.wait_ge(SM["s_wmm"], e + 2)
                            v.tensor_copy(wrow[0:1, :], ps_w[0:1, 256:256 + CAP]).then_inc(SM["s_wrow"], 1)
                        for c in range(CT0G):
                            m = CT0G * e + c
                            par = m % 2
                            v.wait_ge(SM["s_sil"], m + 1)
                            v.wait_ge(SM["s_pu"], m + 1)
                            v.tensor_tensor(hg[:, c * CAP:(c + 1) * CAP],
                                            hg[:, c * CAP:(c + 1) * CAP],
                                            ps_u[par][:, 0:CAP], ALU.mult).then_inc(SM["s_hw"], 1)
                        if e + 1 < E:
                            v.wait_ge(SM["s_wc"], 2 * (e + 2))
                            v.tensor_copy(wcol[:, ((e + 1) % 2) * 2:((e + 1) % 2) * 2 + 2],
                                          ps_w[:, 0:2]).then_inc(SM["s_wsl"], 1)
                        if e % 2 == 1:
                            for j in range(SCN):
                                hct, tc = j // NTC, j % NTC
                                par = j % 2
                                v.wait_ge(SM["s_scp"], SCN * q + j + 1)
                                sl = slice(hct * T + tc * NS, hct * T + tc * NS + NS)
                                v.tensor_tensor(x1[:, sl], x1[:, sl], ps_gs[par][:, 0:NS],
                                                ALU.add).then_inc(SM["s_scadd"], 1)

            with nc.Block("store") as b:
                @b.sync
                def _(sy):
                    for ct in range(CT):
                        sy.dma_start(outT_v[ct], x1[:, ct * T:(ct + 1) * T]).then_inc(SM["s_out"], 16)
                    sy.wait_ge(SM["s_out"], 16 * CT)
                    if dbg:
                        sy.wait_ge(SM["s_dbg"], 16 * 11)

    return nc


def _consts():
    idb = np.eye(128, dtype=np.float32).astype(ml_dtypes.bfloat16)
    idf = np.eye(128, dtype=np.float32)
    onesf = np.ones((128, 128), dtype=np.float32)
    r = np.arange(128)
    bd = (r[:, None] % 8 == r[None, :] % 8).astype(np.float32).astype(ml_dtypes.bfloat16)
    ltf = np.triu(np.ones((128, 128), dtype=np.float32), k=1)
    io = np.broadcast_to(np.arange(CAP, dtype=np.float32), (128, CAP)).astype(ml_dtypes.bfloat16)
    ioc = np.arange(128, dtype=np.float32).reshape(128, 1).copy()
    return idb, idf, onesf, bd, ltf, io, ioc


def kernel(L0_attn_x, L0_attn_Wq, L0_attn_Wk, L0_attn_Wv, L0_attn_Wo,
           L0_ffn_router_w, L0_ffn_gate_w, L0_ffn_up_w, L0_ffn_down_w):
    NCORES = 8
    T = 1024
    x = np.ascontiguousarray(np.asarray(L0_attn_x, dtype=np.float32))
    B, N, Dm = x.shape
    xr = x.reshape(B * N, Dm)
    wq = np.ascontiguousarray(np.asarray(L0_attn_Wq, dtype=np.float32))
    wk = np.ascontiguousarray(np.asarray(L0_attn_Wk, dtype=np.float32))
    wv = np.ascontiguousarray(np.asarray(L0_attn_Wv, dtype=np.float32))
    wo = np.ascontiguousarray(np.asarray(L0_attn_Wo, dtype=np.float32))
    rwT = np.ascontiguousarray(np.asarray(L0_ffn_router_w, dtype=np.float32).T)
    gT = np.ascontiguousarray(np.asarray(L0_ffn_gate_w, dtype=np.float32).transpose(0, 2, 1))
    uT = np.ascontiguousarray(np.asarray(L0_ffn_up_w, dtype=np.float32).transpose(0, 2, 1))
    dTm = np.ascontiguousarray(np.asarray(L0_ffn_down_w, dtype=np.float32).transpose(0, 2, 1))
    idb, idf, onesf, bd, ltf, io, ioc = _consts()

    nc = build_nc(T)
    in_maps = []
    for i in range(NCORES):
        shard = xr[i * T:(i + 1) * T]
        in_maps.append(dict(
            xT=np.ascontiguousarray(shard.T),
            wq=wq, wk=wk, wv=wv, wo=wo, rwT=rwT, gT=gT, uT=uT, dT=dTm,
            c_idb=idb, c_idf=idf, c_1f=onesf, c_bd=bd, c_ltf=ltf, c_iota=io, c_ioc=ioc,
        ))
    res = run_bass_kernel_spmd(nc, in_maps, core_ids=list(range(NCORES)))
    outs = [np.ascontiguousarray(r["outT"].T) for r in res.results]
    out = np.concatenate(outs, axis=0).reshape(B, N, Dm)
    return out


# version 13
# speedup vs baseline: 1.0905x; 1.0905x over previous
import numpy as np
import ml_dtypes
from contextlib import ExitStack

import concourse.bass as bass
import concourse.mybir as mybir
from concourse.bass_utils import run_bass_kernel_spmd

F32 = mybir.dt.float32
BF16 = mybir.dt.bfloat16
AF = mybir.ActivationFunctionType
ALU = mybir.AluOpType
AX = mybir.AxisListType

D = 2048
CT = D // 128
H = 16
E = 32
I = 768
ITn = I // 128
TOPK = 4
EPS = 1e-5
BIG = 1.0e5
CAPH = 112
CAP = 2 * CAPH
BIGS = 384.0
CT0G = 768 // 128


def build_nc(T, dbg=False):
    NH = T // 2
    G = NH // 8
    assert NH <= 512 and NH % 128 == 0
    nc = bass.Bass("TRN2", target_bir_lowering=False, detect_race_conditions=False)

    xT = nc.dram_tensor("xT", [D, T], F32, kind="ExternalInput")
    wq = nc.dram_tensor("wq", [D, D], F32, kind="ExternalInput")
    wk = nc.dram_tensor("wk", [D, D], F32, kind="ExternalInput")
    wv = nc.dram_tensor("wv", [D, D], F32, kind="ExternalInput")
    wo = nc.dram_tensor("wo", [D, D], F32, kind="ExternalInput")
    rwT = nc.dram_tensor("rwT", [D, E], F32, kind="ExternalInput")
    gT = nc.dram_tensor("gT", [E, D, I], F32, kind="ExternalInput")
    uT = nc.dram_tensor("uT", [E, D, I], F32, kind="ExternalInput")
    dT = nc.dram_tensor("dT", [E, I, D], F32, kind="ExternalInput")
    c_idb = nc.dram_tensor("c_idb", [128, 128], BF16, kind="ExternalInput")
    c_idf = nc.dram_tensor("c_idf", [128, 128], F32, kind="ExternalInput")
    c_1f = nc.dram_tensor("c_1f", [128, 128], F32, kind="ExternalInput")
    c_bd = nc.dram_tensor("c_bd", [128, 128], BF16, kind="ExternalInput")
    c_ltf = nc.dram_tensor("c_ltf", [128, 128], F32, kind="ExternalInput")
    c_iota = nc.dram_tensor("c_iota", [128, CAP], BF16, kind="ExternalInput")
    c_ioc = nc.dram_tensor("c_ioc", [128, 1], F32, kind="ExternalInput")
    outT = nc.dram_tensor("outT", [D, T], F32, kind="ExternalOutput")
    if dbg:
        d_xn = nc.dram_tensor("d_xn", [128, CT * T], BF16, kind="ExternalOutput")
        d_qt = nc.dram_tensor("d_qt", [128, H * (T // 2)], BF16, kind="ExternalOutput")
        d_kt = nc.dram_tensor("d_kt", [128, H * (T // 2)], BF16, kind="ExternalOutput")
        d_vt = nc.dram_tensor("d_vt", [128, H * (T // 2)], BF16, kind="ExternalOutput")
        d_ot = nc.dram_tensor("d_ot", [128, H * (T // 2)], BF16, kind="ExternalOutput")
        d_x1a = nc.dram_tensor("d_x1a", [128, CT * T], F32, kind="ExternalOutput")
        d_lgt = nc.dram_tensor("d_lgt", [128, (T // 128) * E], F32, kind="ExternalOutput")
        d_ew = nc.dram_tensor("d_ew", [128, (T // 128) * E], F32, kind="ExternalOutput")
        d_selm = nc.dram_tensor("d_selm", [128, (T // 128) * E], F32, kind="ExternalOutput")
        d_mneg = nc.dram_tensor("d_mneg", [128, T // 128], F32, kind="ExternalOutput")
        d_wtm = nc.dram_tensor("d_wtm", [128, (T // 128) * E], BF16, kind="ExternalOutput")

    xT_v = xT.ap().rearrange("(ct p) t -> ct p t", p=128)
    outT_v = outT.ap().rearrange("(ct p) t -> ct p t", p=128)
    wq_v = wq.ap().rearrange("(ct p) j -> p ct j", p=128)
    wk_v = wk.ap().rearrange("(ct p) j -> p ct j", p=128)
    wv_v = wv.ap().rearrange("(ct p) j -> p ct j", p=128)
    wo_v = wo.ap().rearrange("(ct p) j -> p ct j", p=128)
    rwT_v = rwT.ap().rearrange("(ct p) e -> p ct e", p=128)
    gT_v = gT.ap().rearrange("e (ct p) i -> e p ct i", p=128)
    uT_v = uT.ap().rearrange("e (ct p) i -> e p ct i", p=128)
    dT_v = dT.ap().rearrange("e (it p) j -> e p it j", p=128)

    with ExitStack() as outer:
        oe = outer.enter_context
        x1 = oe(nc.sbuf_tensor("x1", [128, CT * T], F32))
        xn = oe(nc.sbuf_tensor("xn", [128, CT * T], BF16))
        idb = oe(nc.sbuf_tensor("idb", [128, 128], BF16))
        idf = oe(nc.sbuf_tensor("idf", [128, 128], F32))
        onesf = oe(nc.sbuf_tensor("onesf", [128, 128], F32))
        bdm = oe(nc.sbuf_tensor("bdm", [128, 128], BF16))
        RW = oe(nc.sbuf_tensor("RW", [128, CT * E], BF16))
        onesb = oe(nc.sbuf_tensor("onesb", [128, 128], BF16))
        ltf = oe(nc.sbuf_tensor("ltf", [128, 128], F32))
        iota = oe(nc.sbuf_tensor("iota", [128, CAP], BF16))
        ioc = oe(nc.sbuf_tensor("ioc", [128, 1], F32))
        xsub = oe(nc.sbuf_tensor("xsub", [128, 512], F32))
        epsb = oe(nc.sbuf_tensor("epsb", [128, 1], F32))
        lg_sb = oe(nc.sbuf_tensor("lg_sb", [E, T], F32))
        lgt = oe(nc.sbuf_tensor("lgt", [128, (T // 128) * E], F32))
        ew = oe(nc.sbuf_tensor("ew", [128, (T // 128) * E], F32))
        ework = oe(nc.sbuf_tensor("ework", [128, (T // 128) * E], F32))
        hitb = oe(nc.sbuf_tensor("hitb", [128, (T // 128) * E], F32))
        selm = oe(nc.sbuf_tensor("selm", [128, (T // 128) * E], F32))
        mneg = oe(nc.sbuf_tensor("mneg", [128, T // 128], F32))
        mwork = oe(nc.sbuf_tensor("mwork", [128, 2 * (T // 128)], F32))
        Wtm = oe(nc.sbuf_tensor("Wtm", [128, (T // 128) * E], BF16))

        semnames = ("s_xin s_cin s_rw s_wb0 s_wb1 s_wb2 s_wb3 s_gu0 s_gu1 s_wbfree s_x2 s_sqmm s_lnsum s_lnsq s_var s_std s_xn "
                    "s_qkvp s_evd s_eva s_gathp s_pS s_pBV s_Sm s_bv s_pO s_ob s_pOT s_ot "
                    "s_wop s_woadd s_lgp s_lgev s_lgtp s_lgtev s_negm s_exp s_wtm s_wtp s_wt "
                    "s_gufree s_dn s_dnfree s_pg s_pu s_sil s_hw s_py s_pyd s_wpe s_wdve s_out s_dbg "
                    "s_xt s_xta s_xtp s_Rp s_Rev s_rsel s_rtp s_rtev s_G s_xgp s_xge "
                    "s_wmm s_wrow s_wbb s_wc s_wsl s_xga s_gtbp s_gtd s_scp s_scadd s_ygev").split()
        SM = {n: oe(nc.semaphore(n)) for n in semnames}

        attn_scope = ExitStack()
        ae = attn_scope.enter_context
        WB = [ae(nc.sbuf_tensor(f"WB{i}", [128, CT * 128], BF16)) for i in range(4)]
        mean_bc = ae(nc.sbuf_tensor("mean_bc", [128, 512], F32))
        rstd_bc = ae(nc.sbuf_tensor("rstd_bc", [128, 512], F32))
        var_bc = ae(nc.sbuf_tensor("var_bc", [128, 512], F32))
        std_bc = ae(nc.sbuf_tensor("std_bc", [128, 512], F32))
        x2b = [ae(nc.sbuf_tensor(f"x2b{i}", [128, 512], BF16)) for i in range(2)]
        xcb = [ae(nc.sbuf_tensor(f"xcb{i}", [128, 512], BF16)) for i in range(2)]

        with nc.Block("b0") as b:
            @b.sync
            def _(sy):
                for ct in range(CT):
                    sy.dma_start(x1[:, ct * T:(ct + 1) * T], xT_v[ct]).then_inc(SM["s_xin"], 16)
                sy.dma_start(idb[:, :], c_idb.ap()).then_inc(SM["s_cin"], 16)
                sy.dma_start(idf[:, :], c_idf.ap()).then_inc(SM["s_cin"], 16)
                sy.dma_start(onesf[:, :], c_1f.ap()).then_inc(SM["s_cin"], 16)
                sy.dma_start(bdm[:, :], c_bd.ap()).then_inc(SM["s_cin"], 16)
                sy.dma_start(ltf[:, :], c_ltf.ap()).then_inc(SM["s_cin"], 16)
                sy.dma_start(iota[:, :], c_iota.ap()).then_inc(SM["s_cin"], 16)
                sy.dma_start(ioc[:, :], c_ioc.ap()).then_inc(SM["s_cin"], 16)
    
            @b.vector
            def _(v):
                v.memset(epsb[:, :], EPS)
                v.memset(onesb[:, :], 1.0)

            @b.gpsimd
            def _(g):
                g.dma_start(RW[:, :].rearrange("p (ct e) -> p ct e", e=E),
                            rwT_v).then_inc(SM["s_rw"], 16)
                for j in range(4):
                    g.dma_start(WB[j][:, :].rearrange("p (ct f) -> p ct f", f=128),
                                wk_v[:, :, j * 128:(j + 1) * 128]).then_inc(SM[f"s_wb{j}"], 16)

        C = dict(wb=4, wbfree=0, x2=0, sqmm=0, lnsum=0, lnsq=0, var=0, std=0, xn=0,
                 qkvp=0, evd=0, eva=0, gathp=0, pS=0, pBV=0, Sm=0, bv=0, pO=0, ob=0,
                 pOT=0, ot=0, wop=0, woadd=0)

        def emit_ln(b, ps_sum, ps_sq, th, NHl, woadd_base=None):
            base_x2 = C["x2"]; base_sq = C["sqmm"]
            base_sum = C["lnsum"]
            base_var = C["var"]; base_std = C["std"]

            @b.scalar
            def _(s):
                if base_x2 == 0:
                    s.wait_ge(SM["s_xin"], 16 * CT)
                for ct in range(CT):
                    if woadd_base is not None:
                        s.wait_ge(SM["s_woadd"], woadd_base + ct + 1)
                    if base_x2 + ct - 1 > 0:
                        s.wait_ge(SM["s_sqmm"], base_sq + ct - 1)
                    sl = slice(ct * T + th * NHl, ct * T + th * NHl + NHl)
                    s.activation(x2b[ct % 2][:, 0:NHl], x1[:, sl], AF.Square)
                    s.copy(xcb[ct % 2][:, 0:NHl], x1[:, sl]).then_inc(SM["s_x2"], 1)
                s.wait_ge(SM["s_var"], base_var + 1)
                s.activation(std_bc[:, 0:NHl], var_bc[:, 0:NHl], AF.Sqrt, bias=epsb[:, 0:1]).then_inc(SM["s_std"], 1)

            @b.tensor
            def _(t):
                if base_x2 == 0:
                    t.wait_ge(SM["s_xin"], 16 * CT)
                    t.wait_ge(SM["s_cin"], 16 * 7)
                if base_var >= 1:
                    t.wait_ge(SM["s_var"], base_var)
                for ct in range(CT):
                    t.wait_ge(SM["s_x2"], base_x2 + ct + 1)
                    i = t.matmul(ps_sum[:, 0:NHl], onesb[:, :], xcb[ct % 2][:, 0:NHl],
                                 start=(ct == 0), stop=(ct == CT - 1))
                    if ct == CT - 1:
                        i.then_inc(SM["s_lnsum"], 1)
                    i = t.matmul(ps_sq[:, 0:NHl], onesb[:, :], x2b[ct % 2][:, 0:NHl],
                                 start=(ct == 0), stop=(ct == CT - 1))
                    i.then_inc(SM["s_sqmm"], 1)

            @b.vector
            def _(v):
                v.wait_ge(SM["s_lnsum"], base_sum + 1)
                v.tensor_scalar(mean_bc[:, 0:NHl], ps_sum[:, 0:NHl], 1.0 / D, None, ALU.mult)
                v.wait_ge(SM["s_sqmm"], base_sq + CT)
                v.tensor_scalar(var_bc[:, 0:NHl], ps_sq[:, 0:NHl], 1.0 / D, None, ALU.mult)
                v.tensor_mul(rstd_bc[:, 0:NHl], mean_bc[:, 0:NHl], mean_bc[:, 0:NHl])
                v.tensor_sub(var_bc[:, 0:NHl], var_bc[:, 0:NHl], rstd_bc[:, 0:NHl]).then_inc(SM["s_var"], 1)
                v.wait_ge(SM["s_std"], base_std + 1)
                v.reciprocal(rstd_bc[:, 0:NHl], std_bc[:, 0:NHl])
                for ct in range(CT):
                    sl = slice(ct * T + th * NHl, ct * T + th * NHl + NHl)
                    v.tensor_sub(xsub[:, 0:NHl], x1[:, sl], mean_bc[:, 0:NHl])
                    v.tensor_mul(xn[:, sl], xsub[:, 0:NHl], rstd_bc[:, 0:NHl]).then_inc(SM["s_xn"], 1)

            C["x2"] += CT; C["sqmm"] += CT; C["lnsum"] += 1
            C["var"] += 1; C["std"] += 1; C["xn"] += CT

        if True:
            QT = ae(nc.sbuf_tensor("QT", [128, H * NH], BF16))
            KT = ae(nc.sbuf_tensor("KT", [128, H * NH], BF16))
            VT = ae(nc.sbuf_tensor("VT", [128, H * NH], BF16))
            OT = ae(nc.sbuf_tensor("OT", [128, H * NH], BF16))
            aQ = [ae(nc.sbuf_tensor(f"aQ{i}", [128, 128], BF16)) for i in range(2)]
            aK = [ae(nc.sbuf_tensor(f"aK{i}", [128, 128], BF16)) for i in range(2)]
            aV = [ae(nc.sbuf_tensor(f"aV{i}", [128, 128], BF16)) for i in range(2)]
            Smb = [ae(nc.sbuf_tensor(f"Smb{i}", [128, 128], BF16)) for i in range(2)]
            bvb = [ae(nc.sbuf_tensor(f"bvb{i}", [128, 128], BF16)) for i in range(2)]
            obb = [ae(nc.sbuf_tensor(f"obb{i}", [128, 128], BF16)) for i in range(2)]

            for th in range(2):
                wviews = [wk_v, wv_v, wq_v]
                dsts = [KT, VT, QT]

                with nc.psum_tensor(f"ps_sum{th}", [128, NH], F32) as ps_sum, \
                     nc.psum_tensor(f"ps_sq{th}", [128, NH], F32) as ps_sq, \
                     nc.psum_tensor(f"ps_k0_{th}", [128, NH], F32) as ps_k0, \
                     nc.psum_tensor(f"ps_k1_{th}", [128, NH], F32) as ps_k1, \
                     nc.Block(f"qkv{th}") as b:
                    ps_k = [ps_k0, ps_k1]
                    if th == 0:
                        emit_ln(b, ps_sum, ps_sq, 0, NH)
                    base_wb = C["wb"]; base_wbfree = C["wbfree"]
                    base_qkvp = C["qkvp"]; base_evd = C["evd"]; base_eva = C["eva"]
                    base_xn = th * CT

                    @b.gpsimd
                    def _(g, th=th):
                        for m in range(48):
                            gidx = th * 64 + m
                            if th == 0 and m < 4:
                                continue
                            w = wviews[m // 16]
                            j = (m % 16) * 128
                            if m >= 4:
                                g.wait_ge(SM["s_qkvp"], th * 48 + m - 3)
                            g.dma_start(WB[gidx % 4][:, :].rearrange("p (ct f) -> p ct f", f=128),
                                        w[:, :, j:j + 128]).then_inc(SM[f"s_wb{gidx % 4}"], 16)

                    @b.tensor
                    def _(t, th=th):
                        for m in range(48):
                            gidx = th * 64 + m
                            t.wait_ge(SM[f"s_wb{gidx % 4}"], 16 * (gidx // 4 + 1))
                            par = m % 2
                            if m >= 2:
                                if par == 0:
                                    t.wait_ge(SM["s_evd"], base_evd + m // 2)
                                else:
                                    t.wait_ge(SM["s_eva"], base_eva + m // 2)
                            for ct in range(CT):
                                if m == 0:
                                    t.wait_ge(SM["s_xn"], base_xn + ct + 1)
                                i = t.matmul(ps_k[par][:, 0:NH],
                                             WB[gidx % 4][:, ct * 128:(ct + 1) * 128],
                                             xn[:, ct * T + th * NH: ct * T + th * NH + NH],
                                             start=(ct == 0), stop=(ct == CT - 1))
                            i.then_inc(SM["s_qkvp"], 1)

                    @b.vector
                    def _(v, th=th):
                        for m in range(0, 48, 2):
                            v.wait_ge(SM["s_qkvp"], base_qkvp + m + 1)
                            dst = dsts[m // 16]
                            jt = m % 16
                            v.tensor_copy(dst[:, jt * NH:(jt + 1) * NH], ps_k[0][:, 0:NH]).then_inc(SM["s_evd"], 1)

                    @b.scalar
                    def _(s, th=th):
                        for m in range(1, 48, 2):
                            s.wait_ge(SM["s_qkvp"], base_qkvp + m + 1)
                            dst = dsts[m // 16]
                            jt = m % 16
                            s.copy(dst[:, jt * NH:(jt + 1) * NH], ps_k[1][:, 0:NH]).then_inc(SM["s_eva"], 1)

                    if th == 0:
                        emit_ln(b, ps_sum, ps_sq, 1, NH)
                    C["qkvp"] += 48; C["evd"] += 24; C["eva"] += 24

                with nc.psum_tensor(f"ps_S{th}_0", [128, 128], F32) as ps_S0, \
                     nc.psum_tensor(f"ps_S{th}_1", [128, 128], F32) as ps_S1, \
                     nc.psum_tensor(f"ps_BV{th}_0", [128, 128], BF16) as ps_BV0, \
                     nc.psum_tensor(f"ps_BV{th}_1", [128, 128], BF16) as ps_BV1, \
                     nc.psum_tensor(f"ps_O{th}_0", [128, 128], F32) as ps_O0, \
                     nc.psum_tensor(f"ps_O{th}_1", [128, 128], F32) as ps_O1, \
                     nc.psum_tensor(f"ps_OT{th}_0", [128, 128], BF16) as ps_OT0, \
                     nc.psum_tensor(f"ps_OT{th}_1", [128, 128], BF16) as ps_OT1, \
                     nc.Block(f"eins{th}") as b:
                    ps_S = [ps_S0, ps_S1]; ps_BV = [ps_BV0, ps_BV1]
                    ps_O = [ps_O0, ps_O1]; ps_OT = [ps_OT0, ps_OT1]
                    bg = dict(C)

                    QT3 = QT[:, :].rearrange("p (h t) -> p h t", t=NH)
                    KT3 = KT[:, :].rearrange("p (h t) -> p h t", t=NH)
                    VT3 = VT[:, :].rearrange("p (h t) -> p h t", t=NH)
                    OT3 = OT[:, :].rearrange("p (h t) -> p h t", t=NH)

                    if dbg and th == 0:
                        @b.sync
                        def _(sy):
                            sy.dma_start(d_xn.ap(), xn[:, :]).then_inc(SM["s_dbg"], 16)
                            sy.dma_start(d_qt.ap(), QT[:, :]).then_inc(SM["s_dbg"], 16)
                            sy.dma_start(d_kt.ap(), KT[:, :]).then_inc(SM["s_dbg"], 16)
                            sy.dma_start(d_vt.ap(), VT[:, :]).then_inc(SM["s_dbg"], 16)

                    @b.gpsimd
                    def _(g, th=th):
                        for k in range(4):
                            gidx = th * 64 + 48 + k
                            g.dma_start(WB[gidx % 4][:, :].rearrange("p (ct f) -> p ct f", f=128),
                                        wo_v[:, :, k * 128:(k + 1) * 128]).then_inc(SM[f"s_wb{gidx % 4}"], 16)
                        for gi in range(G):
                            p = gi % 2
                            if gi >= 2:
                                g.wait_ge(SM["s_pS"], bg["pS"] + gi - 1)
                                g.wait_ge(SM["s_pBV"], bg["pBV"] + gi - 1)
                            sl = slice(gi * 8, gi * 8 + 8)
                            g.tensor_copy(aQ[p][:, :].rearrange("p (h t) -> p h t", t=8), QT3[:, :, sl])
                            g.tensor_copy(aK[p][:, :].rearrange("p (h t) -> p h t", t=8), KT3[:, :, sl])
                            g.tensor_copy(aV[p][:, :].rearrange("p (h t) -> p h t", t=8), VT3[:, :, sl]).then_inc(SM["s_gathp"], 1)

                    @b.tensor
                    def _(t, th=th):
                        for gi in range(G + 2):
                            if gi < G:
                                p = gi % 2
                                t.wait_ge(SM["s_gathp"], bg["gathp"] + gi + 1)
                                if gi >= 2:
                                    t.wait_ge(SM["s_Sm"], bg["Sm"] + gi - 1)
                                t.matmul(ps_S[p][:, :], aK[p][:, :], aQ[p][:, :],
                                         start=True, stop=True).then_inc(SM["s_pS"], 1)
                                if gi >= 2:
                                    t.wait_ge(SM["s_bv"], bg["bv"] + gi - 1)
                                t.transpose(ps_BV[p][:, :], aV[p][:, :], idb[:, :]).then_inc(SM["s_pBV"], 1)
                            jt = gi - 2
                            if 0 <= jt < G:
                                q = jt % 2
                                t.wait_ge(SM["s_ob"], bg["ob"] + jt + 1)
                                if jt >= 2:
                                    t.wait_ge(SM["s_ot"], bg["ot"] + jt - 1)
                                t.transpose(ps_OT[q][:, :], obb[q][:, :], idb[:, :]).then_inc(SM["s_pOT"], 1)
                            j = gi - 1
                            if 0 <= j < G:
                                q = j % 2
                                t.wait_ge(SM["s_Sm"], bg["Sm"] + j + 1)
                                t.wait_ge(SM["s_bv"], bg["bv"] + j + 1)
                                if j >= 2:
                                    t.wait_ge(SM["s_ob"], bg["ob"] + j - 1)
                                t.matmul(ps_O[q][:, :], Smb[q][:, :], bvb[q][:, :],
                                         start=True, stop=True).then_inc(SM["s_pO"], 1)

                    @b.vector
                    def _(v, th=th):
                        for gi in range(G):
                            p = gi % 2
                            v.wait_ge(SM["s_pS"], bg["pS"] + gi + 1)
                            if gi >= 2:
                                v.wait_ge(SM["s_pO"], bg["pO"] + gi - 1)
                            v.tensor_mul(Smb[p][:, :], ps_S[p][:, :], bdm[:, :]).then_inc(SM["s_Sm"], 1)
                            v.wait_ge(SM["s_pOT"], bg["pOT"] + gi + 1)
                            v.tensor_copy(OT3[:, :, gi * 8:gi * 8 + 8],
                                          ps_OT[p][:, :].rearrange("p (h t) -> p h t", t=8)).then_inc(SM["s_ot"], 1)

                    @b.scalar
                    def _(s, th=th):
                        for gi in range(G):
                            p = gi % 2
                            s.wait_ge(SM["s_pBV"], bg["pBV"] + gi + 1)
                            if gi >= 2:
                                s.wait_ge(SM["s_pO"], bg["pO"] + gi - 1)
                            s.copy(bvb[p][:, :], ps_BV[p][:, :]).then_inc(SM["s_bv"], 1)
                            s.wait_ge(SM["s_pO"], bg["pO"] + gi + 1)
                            if gi >= 2:
                                s.wait_ge(SM["s_pOT"], bg["pOT"] + gi - 1)
                            s.copy(obb[p][:, :], ps_O[p][:, :]).then_inc(SM["s_ob"], 1)

                    C["wb"] += 4
                    C["gathp"] += G; C["pS"] += G; C["pBV"] += G; C["Sm"] += G
                    C["bv"] += G; C["pO"] += G; C["ob"] += G; C["pOT"] += G; C["ot"] += G

                with nc.psum_tensor(f"ps_wo{th}_0", [128, NH], F32) as ps_wo0, \
                     nc.psum_tensor(f"ps_wo{th}_1", [128, NH], F32) as ps_wo1, \
                     nc.psum_tensor(f"ps2_sum{th}", [128, NH], F32) as ps_sum2, \
                     nc.psum_tensor(f"ps2_sq{th}", [128, NH], F32) as ps_sq2, \
                     nc.Block(f"wo{th}") as b:
                    ps_wo = [ps_wo0, ps_wo1]
                    base_wop = C["wop"]; base_woadd = C["woadd"]

                    if dbg and th == 0:
                        @b.sync
                        def _(sy):
                            sy.dma_start(d_ot.ap(), OT[:, :]).then_inc(SM["s_dbg"], 16)

                    @b.gpsimd
                    def _(g, th=th):
                        for k in range(4, 16):
                            gidx = th * 64 + 48 + k
                            g.wait_ge(SM["s_wop"], th * 16 + k - 3)
                            g.dma_start(WB[gidx % 4][:, :].rearrange("p (ct f) -> p ct f", f=128),
                                        wo_v[:, :, k * 128:(k + 1) * 128]).then_inc(SM[f"s_wb{gidx % 4}"], 16)

                    @b.tensor
                    def _(t, th=th):
                        for jt in range(16):
                            gidx = th * 64 + 48 + jt
                            t.wait_ge(SM[f"s_wb{gidx % 4}"], 16 * (gidx // 4 + 1))
                            par = jt % 2
                            if jt >= 2:
                                t.wait_ge(SM["s_woadd"], base_woadd + jt - 1)
                            for ct in range(CT):
                                i = t.matmul(ps_wo[par][:, 0:NH],
                                             WB[gidx % 4][:, ct * 128:(ct + 1) * 128],
                                             OT[:, ct * NH:(ct + 1) * NH],
                                             start=(ct == 0), stop=(ct == CT - 1))
                            i.then_inc(SM["s_wop"], 1)

                    @b.vector
                    def _(v, th=th):
                        for jt in range(16):
                            par = jt % 2
                            v.wait_ge(SM["s_wop"], base_wop + jt + 1)
                            sl = slice(jt * T + th * NH, jt * T + th * NH + NH)
                            v.tensor_tensor(x1[:, sl], x1[:, sl], ps_wo[par][:, 0:NH],
                                            ALU.add).then_inc(SM["s_woadd"], 1)

                    emit_ln(b, ps_sum2, ps_sq2, th, NH, woadd_base=base_woadd)
                    C["wop"] += 16; C["woadd"] += 16

        attn_scope.close()

        NHl = NH
        TT = T // 128
        with nc.psum_tensor("ps_lg", [E, NH], F32) as ps_lg, \
             nc.psum_tensor("ps_lgt", [128, E], F32) as ps_lgt, \
             nc.Block("router") as b:
            br = dict(C)
            base_xn2 = C["xn"] - 2 * CT

            @b.tensor
            def _(t):
                t.wait_ge(SM["s_rw"], 16)
                for thl in range(2):
                    t.wait_ge(SM["s_xn"], base_xn2 + CT * (thl + 1))
                    for ct in range(CT):
                        i = t.matmul(ps_lg[:, 0:NH], RW[:, ct * E:(ct + 1) * E],
                                     xn[:, ct * T + thl * NH: ct * T + thl * NH + NH],
                                     start=(ct == 0), stop=(ct == CT - 1))
                    i.then_inc(SM["s_lgp"], 1)
                    t.wait_ge(SM["s_lgev"], thl + 1)
                for tt in range(TT):
                    if tt >= 1:
                        t.wait_ge(SM["s_lgtev"], tt)
                    t.transpose(ps_lgt[:, :], lg_sb[:, tt * 128:(tt + 1) * 128],
                                idf[0:E, 0:E]).then_inc(SM["s_lgtp"], 1)

            @b.vector
            def _(v):
                def spacer():
                    v.tensor_copy(xsub[:, 0:256], x1[:, 0:256])

                def bcast(ap2d):
                    return ap2d.unsqueeze(2).broadcast_to((128, TT, E))

                lgt3 = lgt[:, :].rearrange("p (tt e) -> p tt e", e=E)
                ework3 = ework[:, :].rearrange("p (tt e) -> p tt e", e=E)
                wselb = hitb
                wselb3 = wselb[:, :].rearrange("p (tt e) -> p tt e", e=E)
                for thl in range(2):
                    v.wait_ge(SM["s_lgp"], thl + 1)
                    v.tensor_copy(lg_sb[:, thl * NH:(thl + 1) * NH], ps_lg[:, 0:NH]).then_inc(SM["s_lgev"], 1)
                v.memset(selm[:, :], 0.0)
                for tt in range(TT):
                    v.wait_ge(SM["s_lgtp"], tt + 1)
                    v.tensor_copy(lgt[:, tt * E:(tt + 1) * E], ps_lgt[:, :]).then_inc(SM["s_lgtev"], 1)
                spacer()
                v.reduce_max(mneg[:, 0:TT], lgt3, axis=AX.X, negate=True)
                spacer()
                v.tensor_tensor(ework3, lgt3, bcast(mneg[:, 0:TT]), ALU.add).then_inc(SM["s_negm"], 1)
                v.wait_ge(SM["s_exp"], 1)
                v.tensor_copy(ework[:, :], ew[:, :])
                for r in range(TOPK):
                    v.reduce_max(mwork[:, 0:TT], ework3, axis=AX.X)
                    spacer()
                    v.tensor_tensor(hitb[:, :].rearrange("p (tt e) -> p tt e", e=E),
                                    ework3, bcast(mwork[:, 0:TT]), ALU.is_ge)
                    v.tensor_add(selm[:, :], selm[:, :], hitb[:, :])
                    v.scalar_tensor_tensor(ework[:, :], hitb[:, :], -BIG, ework[:, :],
                                           ALU.mult, ALU.add)
                v.tensor_mul(wselb[:, :], ew[:, :], selm[:, :])
                v.reduce_sum(mwork[:, TT:2 * TT], wselb3, axis=AX.X)
                spacer()
                v.reciprocal(mwork[:, 0:TT], mwork[:, TT:2 * TT])
                spacer()
                v.tensor_tensor(Wtm[:, :].rearrange("p (tt e) -> p tt e", e=E),
                                wselb3, bcast(mwork[:, 0:TT]), ALU.mult).then_inc(SM["s_wtm"], 1)

            @b.scalar
            def _(s):
                s.wait_ge(SM["s_negm"], 1)
                s.activation(ew[:, :], ework[:, :], AF.Exp).then_inc(SM["s_exp"], 1)


        NS = min(512, T)
        NTC = T // NS
        with ExitStack() as moe_scope:
            me = moe_scope.enter_context
            GT = [me(nc.sbuf_tensor(f"GT{i}", [128, CT * 128], BF16)) for i in range(2)]
            UT = [me(nc.sbuf_tensor(f"UT{i}", [128, CT * 128], BF16)) for i in range(2)]
            DT = me(nc.sbuf_tensor("DT", [128, ITn * D], BF16))
            Xg = me(nc.sbuf_tensor("Xg", [128, CT * CAP], BF16))
            hg = me(nc.sbuf_tensor("hg", [128, ITn * CAP], BF16))
            Gb = [me(nc.sbuf_tensor(f"Gb{i}", [128, TT * CAPH], BF16)) for i in range(2)]
            GTb = [[me(nc.sbuf_tensor(f"GTb{i}_{s2}", [128, T // 2], BF16)) for s2 in range(2)] for i in range(2)]
            Ygb = [[me(nc.sbuf_tensor(f"Ygb{i}_{s2}", [128, D], BF16)) for s2 in range(2)] for i in range(2)]
            wcol = me(nc.sbuf_tensor("wcol", [128, 4], F32))
            Rself = me(nc.sbuf_tensor("Rself", [128, TT * E], F32))
            RT = me(nc.sbuf_tensor("RT", [E, T], BF16))
            wrow = me(nc.sbuf_tensor("wrow", [1, CAP], BF16))

            with nc.psum_tensor("ps_t0", [128, 128], BF16) as ps_t0, \
                 nc.psum_tensor("ps_t1", [128, 128], BF16) as ps_t1, \
                 nc.psum_tensor("ps_R0", [128, E], F32) as ps_R0, \
                 nc.psum_tensor("ps_R1", [128, E], F32) as ps_R1, \
                 nc.psum_tensor("ps_rt0", [E, 128], F32) as ps_rt0, \
                 nc.psum_tensor("ps_rt1", [E, 128], F32) as ps_rt1, \
                 nc.Block("sprep") as b:
                ps_t = [ps_t0, ps_t1]
                ps_R = [ps_R0, ps_R1]
                ps_rt = [ps_rt0, ps_rt1]

                if dbg:
                    @b.sync
                    def _(sy):
                        sy.dma_start(d_x1a.ap(), x1[:, :]).then_inc(SM["s_dbg"], 16)
                        sy.dma_start(d_lgt.ap(), lgt[:, :]).then_inc(SM["s_dbg"], 16)
                        sy.dma_start(d_ew.ap(), ew[:, :]).then_inc(SM["s_dbg"], 16)
                        sy.dma_start(d_selm.ap(), selm[:, :]).then_inc(SM["s_dbg"], 16)
                        sy.dma_start(d_mneg.ap(), mneg[:, :]).then_inc(SM["s_dbg"], 16)
                        sy.dma_start(d_wtm.ap(), Wtm[:, :]).then_inc(SM["s_dbg"], 16)

                @b.gpsimd
                def _(g):
                    for k in range(2):
                        g.dma_start(GT[k][:, :].rearrange("p (ct f) -> p ct f", f=128),
                                    gT_v[0][:, :, k * 128:(k + 1) * 128]).then_inc(SM[f"s_gu{k}"], 16)
                        g.dma_start(UT[k][:, :].rearrange("p (ct f) -> p ct f", f=128),
                                    uT_v[0][:, :, k * 128:(k + 1) * 128]).then_inc(SM[f"s_gu{k}"], 16)
                    g.dma_start(DT[:, :].rearrange("p (it f) -> p it f", f=D),
                                dT_v[0]).then_inc(SM["s_dn"], 16)

                @b.tensor
                def _(t):
                    t.wait_ge(SM["s_cin"], 16 * 7)
                    idx = 0
                    for tt in range(TT):
                        for ct in range(CT):
                            par = idx % 2
                            if idx >= 2:
                                if par == 0:
                                    t.wait_ge(SM["s_xt"], idx // 2)
                                else:
                                    t.wait_ge(SM["s_xta"], idx // 2)
                            t.transpose(ps_t[par][:, :],
                                        xn[:, ct * T + tt * 128: ct * T + tt * 128 + 128],
                                        idb[:, :]).then_inc(SM["s_xtp"], 1)
                            idx += 1
                    for tt in range(TT):
                        par = tt % 2
                        hstart = 0 if tt < TT // 2 else TT // 2
                        if tt >= 2:
                            t.wait_ge(SM["s_Rev"], tt - 1)
                        i = t.matmul(ps_R[par][:, :], ltf[:, :], selm[:, tt * E:(tt + 1) * E],
                                     start=True, stop=(tt == hstart))
                        for tp in range(hstart, tt):
                            i = t.matmul(ps_R[par][:, :], onesf[:, :], selm[:, tp * E:(tp + 1) * E],
                                         start=False, stop=(tp == tt - 1))
                        i.then_inc(SM["s_Rp"], 1)
                    for tt in range(TT):
                        par = tt % 2
                        t.wait_ge(SM["s_rsel"], 1)
                        if tt >= 2:
                            t.wait_ge(SM["s_rtev"], tt - 1)
                        t.transpose(ps_rt[par][:, :], Rself[:, tt * E:(tt + 1) * E],
                                    idf[:, :]).then_inc(SM["s_rtp"], 1)

                @b.vector
                def _(v):
                    v.wait_ge(SM["s_cin"], 16 * 7)
                    idx = 0
                    for tt in range(TT):
                        for ct in range(CT):
                            if idx % 2 == 0:
                                v.wait_ge(SM["s_xtp"], idx + 1)
                                v.tensor_copy(xn[:, ct * T + tt * 128: ct * T + tt * 128 + 128],
                                              ps_t[0][:, :]).then_inc(SM["s_xt"], 1)
                            idx += 1
                    for tt in range(TT):
                        v.wait_ge(SM["s_Rp"], tt + 1)
                        v.tensor_copy(Rself[:, tt * E:(tt + 1) * E], ps_R[tt % 2][:, :]).then_inc(SM["s_Rev"], 1)
                    v.tensor_copy(xsub[:, 0:256], x1[:, 0:256])
                    v.tensor_scalar(Rself[:, :], Rself[:, :], BIGS, None, ALU.add)
                    i = v.scalar_tensor_tensor(Rself[:, :], selm[:, :], -BIGS, Rself[:, :],
                                               ALU.mult, ALU.add)
                    i.then_inc(SM["s_rsel"], 1)
                    for tt in range(TT):
                        v.wait_ge(SM["s_rtp"], tt + 1)
                        v.tensor_copy(RT[0:E, tt * 128:(tt + 1) * 128], ps_rt[tt % 2][:, :]).then_inc(SM["s_rtev"], 1)
                    for tt in range(TT):
                        i = v.tensor_single_scalar(Gb[0][:, tt * CAPH:(tt + 1) * CAPH], iota[:, 0:CAPH],
                                                   Rself[:, (tt * E):(tt * E) + 1], ALU.is_equal)
                    i.then_inc(SM["s_G"], 1)
                    for yi in range(2):
                        for s2 in range(2):
                            v.memset(Ygb[yi][s2][96:128, :], 0.0)

                @b.scalar
                def _(s):
                    idx = 0
                    for tt in range(TT):
                        for ct in range(CT):
                            if idx % 2 == 1:
                                s.wait_ge(SM["s_xtp"], idx + 1)
                                s.copy(xn[:, ct * T + tt * 128: ct * T + tt * 128 + 128],
                                       ps_t[1][:, :]).then_inc(SM["s_xta"], 1)
                            idx += 1

            with nc.psum_tensor("A0", [128, 512], F32) as A0, \
                 nc.psum_tensor("A1", [128, 512], F32) as A1, \
                 nc.psum_tensor("A2", [128, 512], F32) as A2, \
                 nc.psum_tensor("A3", [128, 512], F32) as A3, \
                 nc.psum_tensor("A4", [128, 512], F32) as A4, \
                 nc.psum_tensor("A5", [128, 512], F32) as A5, \
                 nc.psum_tensor("A6", [128, 512], F32) as A6, \
                 nc.psum_tensor("A7", [128, 512], F32) as A7, \
                 nc.Block("smoe") as b:
                ps_gs = [A0, A1]
                ps_g = [A2, A3]
                ps_u = [A4, A5]
                ps_w = A6
                ps_gtb = A7
                SCN = 16 * NTC

                @b.gpsimd
                def _(g):
                    for e in range(E):
                        for c in range(CT0G):
                            k = CT0G * e + c
                            if k < 2:
                                continue
                            g.wait_ge(SM["s_pu"], k - 1)
                            g.dma_start(GT[k % 2][:, :].rearrange("p (ct f) -> p ct f", f=128),
                                        gT_v[e][:, :, c * 128:(c + 1) * 128]).then_inc(SM[f"s_gu{k % 2}"], 16)
                            g.dma_start(UT[k % 2][:, :].rearrange("p (ct f) -> p ct f", f=128),
                                        uT_v[e][:, :, c * 128:(c + 1) * 128]).then_inc(SM[f"s_gu{k % 2}"], 16)
                            if c == 0 and e >= 1:
                                g.wait_ge(SM["s_py"], 8 * e)
                                g.dma_start(DT[:, :].rearrange("p (it f) -> p it f", f=D),
                                            dT_v[e]).then_inc(SM["s_dn"], 16)

                @b.tensor
                def _(t):
                    t.wait_ge(SM["s_rtev"], TT)
                    t.wait_ge(SM["s_G"], 1)
                    t.wait_ge(SM["s_wtm"], 1)
                    for tc in range(NTC):
                        if tc >= 1:
                            t.wait_ge(SM["s_gtd"], tc)
                        t.matmul(ps_gtb[:, 0:NS], idb[0:E, 0:1].broadcast_to((E, 128)),
                                 RT[0:E, tc * NS:(tc + 1) * NS],
                                 start=True, stop=True).then_inc(SM["s_gtbp"], 1)
                    for tt in range(TT):
                        half = 0 if tt < TT // 2 else 1
                        i = t.matmul(ps_w[0:1, 256 + half * CAPH:256 + half * CAPH + CAPH],
                                     Wtm[:, 0:1],
                                     Gb[0][:, tt * CAPH:(tt + 1) * CAPH],
                                     start=(tt == 0), stop=(tt == TT - 1))
                    i.then_inc(SM["s_wmm"], 1)
                    t.wait_ge(SM["s_wrow"], 1)
                    for sblk in range(2):
                        t.matmul(ps_w[0:CAPH, sblk:sblk + 1],
                                 wrow[0:1, sblk * CAPH:(sblk + 1) * CAPH],
                                 onesb[0:1, 0:1],
                                 start=True, stop=True).then_inc(SM["s_wc"], 1)
                    for e in range(E):
                        q = e // 2
                        for ct in range(CT):
                            par = ct % 2
                            gi = 16 * e + ct
                            if e % 2 == 0:
                                if e >= 2 and ct < 2:
                                    t.wait_ge(SM["s_scadd"], SCN * q)
                            if gi - 2 >= 0 and not (e % 2 == 0 and ct < 2):
                                g2 = gi - 2
                                if g2 % 2 == 0:
                                    t.wait_ge(SM["s_xge"], g2 // 2 + 1)
                                else:
                                    t.wait_ge(SM["s_xga"], g2 // 2 + 1)
                            for tt in range(TT):
                                half = 0 if tt < TT // 2 else 1
                                i = t.matmul(ps_gs[par][:, half * CAPH:(half + 1) * CAPH],
                                             xn[:, ct * T + tt * 128: ct * T + tt * 128 + 128],
                                             Gb[e % 2][:, tt * CAPH:(tt + 1) * CAPH],
                                             start=(tt == 0), stop=(tt == TT - 1))
                            i.then_inc(SM["s_xgp"], 1)
                        if e + 1 < E:
                            t.wait_ge(SM["s_G"], e + 2)
                            for tc in range(NTC):
                                t.wait_ge(SM["s_gtd"], NTC * (e + 1) + tc)
                                t.matmul(ps_gtb[:, 0:NS],
                                         idb[0:E, (e + 1):(e + 2)].broadcast_to((E, 128)),
                                         RT[0:E, tc * NS:(tc + 1) * NS],
                                         start=True, stop=True).then_inc(SM["s_gtbp"], 1)
                            t.wait_ge(SM["s_wsl"], e + 1)
                            for tt in range(TT):
                                half = 0 if tt < TT // 2 else 1
                                i = t.matmul(ps_w[0:1, 256 + half * CAPH:256 + half * CAPH + CAPH],
                                             Wtm[:, (tt * E + e + 1):(tt * E + e + 1) + 1],
                                             Gb[(e + 1) % 2][:, tt * CAPH:(tt + 1) * CAPH],
                                             start=(tt == 0), stop=(tt == TT - 1))
                            i.then_inc(SM["s_wmm"], 1)
                        for c in range(CT0G):
                            k = CT0G * e + c
                            t.wait_ge(SM[f"s_gu{k % 2}"], 16 * (k // 2 + 1))
                            m = k
                            par = m % 2
                            if m >= 2:
                                t.wait_ge(SM["s_hw"], m - 1)
                            for ct in range(CT):
                                if c == 0:
                                    gi = 16 * e + ct
                                    if gi % 2 == 0:
                                        t.wait_ge(SM["s_xge"], gi // 2 + 1)
                                    else:
                                        t.wait_ge(SM["s_xga"], gi // 2 + 1)
                                i = t.matmul(ps_g[par][:, 0:CAP],
                                             GT[k % 2][:, ct * 128:(ct + 1) * 128],
                                             Xg[:, ct * CAP:(ct + 1) * CAP],
                                             start=(ct == 0), stop=(ct == CT - 1))
                            i.then_inc(SM["s_pg"], 1)
                            if e >= 1 and c < 2:
                                t.wait_ge(SM["s_ygev"], 8 * e)
                            for ct in range(CT):
                                i = t.matmul(ps_u[par][:, 0:CAP],
                                             UT[k % 2][:, ct * 128:(ct + 1) * 128],
                                             Xg[:, ct * CAP:(ct + 1) * CAP],
                                             start=(ct == 0), stop=(ct == CT - 1))
                            i.then_inc(SM["s_pu"], 1)
                            if c == 2 and e + 1 < E:
                                t.wait_ge(SM["s_wrow"], e + 2)
                                for sblk in range(2):
                                    t.matmul(ps_w[0:CAPH, sblk:sblk + 1],
                                             wrow[0:1, sblk * CAPH:(sblk + 1) * CAPH],
                                             onesb[0:1, 0:1],
                                             start=True, stop=True).then_inc(SM["s_wc"], 1)
                        t.wait_ge(SM["s_dn"], 16 * (e + 1))
                        t.wait_ge(SM["s_hw"], CT0G * (e + 1))
                        for f in range(8):
                            sblk, hc = f // 4, f % 4
                            par = f % 2
                            fy = 8 * e + f
                            if fy >= 2:
                                t.wait_ge(SM["s_ygev"], fy - 1)
                            for it in range(ITn):
                                i = t.matmul(ps_u[par][0:CAPH, 0:512],
                                             hg[:, it * CAP + sblk * CAPH: it * CAP + sblk * CAPH + CAPH],
                                             DT[:, it * D + hc * 512: it * D + hc * 512 + 512],
                                             start=(it == 0), stop=(it == ITn - 1))
                            i.then_inc(SM["s_py"], 1)
                        if e % 2 == 1:
                            t.wait_ge(SM["s_ygev"], 8 * (e + 1))
                            t.wait_ge(SM["s_gtd"], NTC * (e + 1))
                            for j in range(SCN):
                                hct, tc = j // NTC, j % NTC
                                par = j % 2
                                sj = SCN * q + j
                                if j < 2:
                                    lastg = 16 * (e + 1) - 1
                                    t.wait_ge(SM["s_xge"], (lastg - 1) // 2 + 1)
                                    t.wait_ge(SM["s_xga"], lastg // 2 + 1)
                                if sj >= 2:
                                    t.wait_ge(SM["s_scadd"], sj - 1)
                                HLEN = T // 2
                                parts = [(h, max(tc * NS, h * HLEN) - tc * NS,
                                          min((tc + 1) * NS, (h + 1) * HLEN) - max(tc * NS, h * HLEN))
                                         for h in range(2)
                                         if max(tc * NS, h * HLEN) < min((tc + 1) * NS, (h + 1) * HLEN)]
                                first = True
                                for ep in range(2):
                                    for (h, off, ln) in parts:
                                        hb = tc * NS + off - h * HLEN
                                        i = t.matmul(ps_gs[par][:, off:off + ln],
                                                     Ygb[ep][h][:, hct * 128:(hct + 1) * 128],
                                                     GTb[ep][h][:, hb:hb + ln],
                                                     start=first,
                                                     stop=(ep == 1 and (h, off, ln) == parts[-1]))
                                        first = False
                                i.then_inc(SM["s_scp"], 1)

                @b.scalar
                def _(s):
                    for e in range(E):
                        for ct in range(1, CT, 2):
                            gi = 16 * e + ct
                            s.wait_ge(SM["s_xgp"], gi + 1)
                            s.copy(Xg[:, ct * CAP:(ct + 1) * CAP],
                                   ps_gs[ct % 2][:, 0:CAP]).then_inc(SM["s_xga"], 1)
                        for c in range(CT0G):
                            m = CT0G * e + c
                            par = m % 2
                            s.wait_ge(SM["s_pg"], m + 1)
                            s.activation(hg[:, c * CAP:(c + 1) * CAP], ps_g[par][:, 0:CAP],
                                         AF.Silu).then_inc(SM["s_sil"], 1)
                        s.wait_ge(SM["s_wsl"], e + 1)
                        if e >= 2:
                            s.wait_ge(SM["s_scp"], SCN * (e // 2))
                        for f in range(8):
                            sblk, hc = f // 4, f % 4
                            s.wait_ge(SM["s_py"], 8 * e + f + 1)
                            s.activation(Ygb[e % 2][sblk][0:CAPH, hc * 512:(hc + 1) * 512],
                                         ps_u[f % 2][0:CAPH, 0:512], AF.Copy,
                                         scale=wcol[0:CAPH, (e % 2) * 2 + sblk:(e % 2) * 2 + sblk + 1]).then_inc(SM["s_ygev"], 1)

                @b.vector
                def _(v):
                    HLEN = T // 2
                    def hgrp(tc):
                        lo, hi = tc * NS, (tc + 1) * NS
                        return [(h, max(lo, h * HLEN) - lo, min(hi, (h + 1) * HLEN) - max(lo, h * HLEN))
                                for h in range(2) if max(lo, h * HLEN) < min(hi, (h + 1) * HLEN)]
                    for tc in range(NTC):
                        v.wait_ge(SM["s_gtbp"], tc + 1)
                        for (h, off, ln) in hgrp(tc):
                            hb = tc * NS + off - h * HLEN
                            i = v.tensor_single_scalar(GTb[0][h][:, hb:hb + ln],
                                                       ps_gtb[:, off:off + ln],
                                                       ioc[:, 0:1], ALU.is_equal)
                        i.then_inc(SM["s_gtd"], 1)
                    v.wait_ge(SM["s_wmm"], 1)
                    v.tensor_copy(wrow[0:1, :], ps_w[0:1, 256:256 + CAP]).then_inc(SM["s_wrow"], 1)
                    v.wait_ge(SM["s_wc"], 2)
                    v.memset(wcol[:, :], 0.0)
                    v.tensor_copy(wcol[0:CAPH, 0:2], ps_w[0:CAPH, 0:2]).then_inc(SM["s_wsl"], 1)
                    for e in range(E):
                        q = e // 2
                        if e + 1 < E:
                            if e >= 1:
                                v.wait_ge(SM["s_xgp"], 16 * e)
                            for tt in range(TT):
                                i = v.tensor_single_scalar(Gb[(e + 1) % 2][:, tt * CAPH:(tt + 1) * CAPH],
                                                           iota[:, 0:CAPH],
                                                           Rself[:, (tt * E + e + 1):(tt * E + e + 1) + 1],
                                                           ALU.is_equal)
                            i.then_inc(SM["s_G"], 1)
                        for ct in range(0, CT, 2):
                            gi = 16 * e + ct
                            v.wait_ge(SM["s_xgp"], gi + 1)
                            v.tensor_copy(Xg[:, ct * CAP:(ct + 1) * CAP],
                                          ps_gs[ct % 2][:, 0:CAP]).then_inc(SM["s_xge"], 1)
                        if e + 1 < E:
                            for tc in range(NTC):
                                v.wait_ge(SM["s_gtbp"], NTC * (e + 1) + tc + 1)
                                for (h, off, ln) in hgrp(tc):
                                    hb = tc * NS + off - h * HLEN
                                    i = v.tensor_single_scalar(GTb[(e + 1) % 2][h][:, hb:hb + ln],
                                                               ps_gtb[:, off:off + ln],
                                                               ioc[:, 0:1], ALU.is_equal)
                                i.then_inc(SM["s_gtd"], 1)
                            v.wait_ge(SM["s_wmm"], e + 2)
                            v.tensor_copy(wrow[0:1, :], ps_w[0:1, 256:256 + CAP]).then_inc(SM["s_wrow"], 1)
                        for c in range(CT0G):
                            m = CT0G * e + c
                            par = m % 2
                            v.wait_ge(SM["s_sil"], m + 1)
                            v.wait_ge(SM["s_pu"], m + 1)
                            v.tensor_tensor(hg[:, c * CAP:(c + 1) * CAP],
                                            hg[:, c * CAP:(c + 1) * CAP],
                                            ps_u[par][:, 0:CAP], ALU.mult).then_inc(SM["s_hw"], 1)
                        if e + 1 < E:
                            v.wait_ge(SM["s_wc"], 2 * (e + 2))
                            v.tensor_copy(wcol[0:CAPH, ((e + 1) % 2) * 2:((e + 1) % 2) * 2 + 2],
                                          ps_w[0:CAPH, 0:2]).then_inc(SM["s_wsl"], 1)
                        if e % 2 == 1:
                            for j in range(SCN):
                                hct, tc = j // NTC, j % NTC
                                par = j % 2
                                v.wait_ge(SM["s_scp"], SCN * q + j + 1)
                                sl = slice(hct * T + tc * NS, hct * T + tc * NS + NS)
                                v.tensor_tensor(x1[:, sl], x1[:, sl], ps_gs[par][:, 0:NS],
                                                ALU.add).then_inc(SM["s_scadd"], 1)

            with nc.Block("store") as b:
                @b.sync
                def _(sy):
                    for ct in range(CT):
                        sy.dma_start(outT_v[ct], x1[:, ct * T:(ct + 1) * T]).then_inc(SM["s_out"], 16)
                    sy.wait_ge(SM["s_out"], 16 * CT)
                    if dbg:
                        sy.wait_ge(SM["s_dbg"], 16 * 11)

    return nc


def _consts():
    idb = np.eye(128, dtype=np.float32).astype(ml_dtypes.bfloat16)
    idf = np.eye(128, dtype=np.float32)
    onesf = np.ones((128, 128), dtype=np.float32)
    r = np.arange(128)
    bd = (r[:, None] % 8 == r[None, :] % 8).astype(np.float32).astype(ml_dtypes.bfloat16)
    ltf = np.triu(np.ones((128, 128), dtype=np.float32), k=1)
    io = np.broadcast_to(np.arange(CAP, dtype=np.float32), (128, CAP)).astype(ml_dtypes.bfloat16)
    ioc = np.arange(128, dtype=np.float32).reshape(128, 1).copy()
    return idb, idf, onesf, bd, ltf, io, ioc


def kernel(L0_attn_x, L0_attn_Wq, L0_attn_Wk, L0_attn_Wv, L0_attn_Wo,
           L0_ffn_router_w, L0_ffn_gate_w, L0_ffn_up_w, L0_ffn_down_w):
    NCORES = 8
    T = 1024
    x = np.ascontiguousarray(np.asarray(L0_attn_x, dtype=np.float32))
    B, N, Dm = x.shape
    xr = x.reshape(B * N, Dm)
    wq = np.ascontiguousarray(np.asarray(L0_attn_Wq, dtype=np.float32))
    wk = np.ascontiguousarray(np.asarray(L0_attn_Wk, dtype=np.float32))
    wv = np.ascontiguousarray(np.asarray(L0_attn_Wv, dtype=np.float32))
    wo = np.ascontiguousarray(np.asarray(L0_attn_Wo, dtype=np.float32))
    rwT = np.ascontiguousarray(np.asarray(L0_ffn_router_w, dtype=np.float32).T)
    gT = np.ascontiguousarray(np.asarray(L0_ffn_gate_w, dtype=np.float32).transpose(0, 2, 1))
    uT = np.ascontiguousarray(np.asarray(L0_ffn_up_w, dtype=np.float32).transpose(0, 2, 1))
    dTm = np.ascontiguousarray(np.asarray(L0_ffn_down_w, dtype=np.float32).transpose(0, 2, 1))
    idb, idf, onesf, bd, ltf, io, ioc = _consts()

    nc = build_nc(T)
    in_maps = []
    for i in range(NCORES):
        shard = xr[i * T:(i + 1) * T]
        in_maps.append(dict(
            xT=np.ascontiguousarray(shard.T),
            wq=wq, wk=wk, wv=wv, wo=wo, rwT=rwT, gT=gT, uT=uT, dT=dTm,
            c_idb=idb, c_idf=idf, c_1f=onesf, c_bd=bd, c_ltf=ltf, c_iota=io, c_ioc=ioc,
        ))
    res = run_bass_kernel_spmd(nc, in_maps, core_ids=list(range(NCORES)))
    outs = [np.ascontiguousarray(r["outT"].T) for r in res.results]
    out = np.concatenate(outs, axis=0).reshape(B, N, Dm)
    return out
